# version 17
# baseline (speedup 1.0000x reference)
import math
from contextlib import ExitStack

import numpy as np
import ml_dtypes

import concourse.bass as bass
import concourse.mybir as mybir
from concourse.bass_utils import run_bass_kernel_spmd

F32 = mybir.dt.float32
BF16 = mybir.dt.bfloat16
I32 = mybir.dt.int32
AF = mybir.ActivationFunctionType
OP = mybir.AluOpType
AX = mybir.AxisListType

BF = ml_dtypes.bfloat16

B, T, D = 8, 2048, 1024
H, DH = 16, 64
HID = 3072
NE = 8
CH = D // 128
TT = T // 128
TC = 512
NTC = T // TC
ATT_SCALE = 0.125
EPS = 1e-5

ENGINES = ("tensor", "scalar", "vector", "gpsimd", "sync")
DMA_Q = {"sync": "Dsy", "gpsimd": "Dgp"}
NDMASEM = 8



class Reg:
    __slots__ = ("space", "p0", "p1", "a0", "a1")

    def __init__(self, space, p0, p1, a0, a1):
        self.space, self.p0, self.p1, self.a0, self.a1 = space, p0, p1, a0, a1

    def overlaps(self, o):
        return (self.space == o.space and self.p0 < o.p1 and o.p0 < self.p1
                and self.a0 < o.a1 and o.a0 < self.a1)

    def covers(self, o):
        return (self.space == o.space and self.p0 <= o.p0 and self.p1 >= o.p1
                and self.a0 <= o.a0 and self.a1 >= o.a1)


class Rec:
    __slots__ = ("kind", "reg", "eng", "token", "clock")

    def __init__(self, kind, reg, eng, token, clock):
        self.kind, self.reg, self.eng, self.token, self.clock = (
            kind, reg, eng, token, clock)


class Buf:

    def __init__(self, prog, handle, space):
        self.prog = prog
        self.h = handle
        self.space = space
        self.P = handle.shape[0]
        self.F = int(np.prod(handle.shape[1:]))
        self.esz = mybir.dt.size(handle.dtype)
        mloc = prog.nc.lookup_mloc(handle)
        if space == "P":
            self.addr = int(mloc.bank) * 2048
        else:
            a = getattr(mloc, "addr", None)
            self.addr = a if a is not None else id(handle)

    def __getitem__(self, idx):
        ps, fs = idx
        if isinstance(ps, slice):
            p0, p1 = ps.start or 0, (ps.stop if ps.stop is not None else self.P)
        else:
            p0, p1 = ps, ps + 1
        if isinstance(fs, slice):
            f0, f1 = fs.start or 0, (fs.stop if fs.stop is not None else self.F)
        else:
            f0, f1 = fs, fs + 1
        return View(self, p0, p1, f0, f1)

    @property
    def v(self):
        return self[:, :]


class View:
    __slots__ = ("buf", "p0", "p1", "f0", "f1", "_apo")

    def __init__(self, buf, p0, p1, f0, f1, apo=None):
        self.buf, self.p0, self.p1, self.f0, self.f1 = buf, p0, p1, f0, f1
        self._apo = apo

    def with_ap(self, ap):
        return View(self.buf, self.p0, self.p1, self.f0, self.f1, apo=ap)

    @property
    def ap(self):
        if self._apo is not None:
            return self._apo
        h = self.buf.h
        full = h.ap()
        if len(h.shape) > 2:
            names = " ".join(f"a{i}" for i in range(1, len(h.shape)))
            full = full.rearrange(f"a0 {names} -> a0 ({names})")
        return full[self.p0:self.p1, self.f0:self.f1]

    def reg(self):
        b = self.buf
        if b.space == "P":
            return Reg("P", 0, 128, b.addr, b.addr + b.F * b.esz)
        return Reg(b.space, self.p0, self.p1,
                   b.addr + self.f0 * b.esz, b.addr + self.f1 * b.esz)


class DView:
    __slots__ = ("ap", "_reg")

    def __init__(self, ap, name=None):
        self.ap = ap
        nm = name or ap.tensor.name
        self._reg = Reg("D:" + nm, 0, 128, 0, 1 << 60)

    def reg(self):
        return self._reg


class Prog:
    def __init__(self, nc, ctx):
        self.nc = nc
        self.ctx = ctx
        self.steps = {e: [] for e in ENGINES}
        self.sem = {}
        self.dma_sems = set()
        for nm in ["Et", "Es", "Ev", "Eg"]:
            self.sem[nm] = ctx.enter_context(nc.semaphore(nm))
        for q in DMA_Q.values():
            for k in range(NDMASEM):
                nm = f"{q}{k}"
                self.sem[nm] = ctx.enter_context(nc.semaphore(nm))
                self.dma_sems.add(nm)
        self.esem = {"tensor": "Et", "scalar": "Es", "vector": "Ev",
                     "gpsimd": "Eg"}
        self.cnt = {nm: 0 for nm in self.sem}
        self.dma_rr = {q: 0 for q in DMA_Q.values()}
        self.dma_last = {}
        self.clock = {e: {} for e in ENGINES}
        self.recs = []
        self.n_ops = 0

    def _emit(self, eng, build, reads, writes, token_sem, token_val, mm=False,
              extra_deps=()):
        regs_r = [v.reg() for v in reads if v is not None]
        regs_w = [v.reg() for v in writes if v is not None]
        deps = list(extra_deps)
        for rec in self.recs:
            if rec.kind == "W":
                hit = any(rec.reg.overlaps(r) for r in regs_r) or \
                    any(rec.reg.overlaps(r) for r in regs_w)
            else:
                hit = any(rec.reg.overlaps(r) for r in regs_w)
                if not hit and rec.reg.space == "P" and rec.eng != eng:
                    hit = any(rec.reg.overlaps(r) for r in regs_r)
            if hit:
                deps.append(rec)
        clk = self.clock[eng]
        waits = []
        for rec in deps:
            if mm and rec.eng == "tensor" and eng == "tensor":
                continue
            s, v = rec.token
            if clk.get(s, 0) >= v:
                continue
            waits.append((s, v))
            for cs, cv in rec.clock.items():
                if clk.get(cs, 0) < cv:
                    clk[cs] = cv
            clk[s] = max(clk.get(s, 0), v)
        self.steps[eng].append((waits, build, (token_sem, token_val)))
        snap = dict(clk)
        snap[token_sem] = token_val
        tok = (token_sem, token_val)
        new_recs = []
        for rec in self.recs:
            if any(w.covers(rec.reg) for w in regs_w):
                continue
            new_recs.append(rec)
        me = "tensor" if eng == "tensor" else eng
        for r in regs_w:
            new_recs.append(Rec("W", r, me, tok, snap))
        for r in regs_r:
            new_recs.append(Rec("R", r, me, tok, snap))
        self.recs = new_recs
        self.n_ops += 1

    def op(self, eng, build, reads=(), writes=(), mm=False):
        sname = self.esem[eng]
        self.cnt[sname] += 1
        self._emit(eng, build, reads, writes, sname, self.cnt[sname], mm=mm)

    def dma(self, eng, out, in_, **kw):
        q = DMA_Q[eng]
        slot = self.dma_rr[q]
        self.dma_rr[q] = (slot + 1) % NDMASEM
        sname = f"{q}{slot}"
        self.cnt[sname] += 16
        val = self.cnt[sname]
        out_ap = out.ap if isinstance(out, (View, DView)) else out
        in_ap = in_.ap if isinstance(in_, (View, DView)) else in_

        def build(e, _o=out_ap, _i=in_ap, _kw=kw):
            return e.dma_start(out=_o, in_=_i, **_kw)
        reads = [in_] if isinstance(in_, (View, DView)) else []
        writes = [out] if isinstance(out, (View, DView)) else []
        extra = []
        if sname in self.dma_last:
            pv, pclk = self.dma_last[sname]
            extra.append(Rec("W", Reg("NONE", 0, 0, 0, 0), "dma",
                             (sname, pv), pclk))
        self._emit(eng, build, reads, writes, sname, val, extra_deps=extra)
        snap = dict(self.clock[eng])
        snap[sname] = val
        self.dma_last[sname] = (val, snap)

    def barrier_all(self, eng="sync"):
        waits = []
        for s in self.dma_sems:
            if self.cnt[s]:
                waits.append((s, self.cnt[s]))
        for s in ("Et", "Es", "Ev", "Eg"):
            if self.cnt[s]:
                waits.append((s, self.cnt[s]))
        self.steps[eng].append((waits, None, None))

    def realize(self):
        nc = self.nc
        dmaq = self.dma_sems
        with nc.Block() as block:
            for eng in ENGINES:
                steps = self.steps[eng]
                if not steps:
                    continue

                def runner(e, _steps=steps):
                    for waits, build, tok in _steps:
                        for s, v in waits:
                            e.wait_ge(self.sem[s], v)
                        if build is not None:
                            ins = build(e)
                            if tok is not None:
                                ins.then_inc(
                                    self.sem[tok[0]],
                                    16 if tok[0] in dmaq else 1)
                getattr(block, eng)(runner)

    def mm(self, out, lhsT, rhs, start=True, stop=True):
        def build(e, o=out.ap, l=lhsT.ap, r=rhs.ap, st=start, sp=stop):
            return e.matmul(o, l, r, start=st, stop=sp)
        self.op("tensor", build, reads=[lhsT, rhs], writes=[out], mm=True)

    def act(self, out, in_, func, bias=0.0, scale=1.0, accum=None):
        reads = [in_]
        b, s = bias, scale
        if isinstance(bias, View):
            reads.append(bias)
            b = bias.ap
            if func == AF.Copy:
                func = AF.Identity
        if isinstance(scale, View):
            reads.append(scale)
            s = scale.ap
        writes = [out] + ([accum] if accum is not None else [])

        def build(e, o=out.ap, i=in_.ap, f=func, b=b, s=s,
                  a=(accum.ap if accum is not None else None)):
            return e.activation(o, i, f, bias=b, scale=s, accum_out=a)
        self.op("scalar", build, reads=reads, writes=writes)

    def copy(self, eng, out, in_):
        def build(e, o=out.ap, i=in_.ap):
            return e.tensor_copy(o, i)
        self.op(eng, build, reads=[in_], writes=[out])

    def tt(self, eng, out, a, b, op):
        def build(e, o=out.ap, x=a.ap, y=b.ap, op=op):
            return e.tensor_tensor(o, x, y, op)
        self.op(eng, build, reads=[a, b], writes=[out])

    def ts(self, eng, out, a, scalar, op, accum=None):
        reads = [a]
        s = scalar
        if isinstance(scalar, View):
            reads.append(scalar)
            s = scalar.ap
        writes = [out] + ([accum] if accum is not None else [])

        def build(e, o=out.ap, x=a.ap, s=s, op=op,
                  ac=(accum.ap if accum is not None else None)):
            return e.tensor_scalar(o, x, s, None, op, accum_out=ac)
        self.op(eng, build, reads=reads, writes=writes)

    def stt(self, eng, out, in0, scalar, in1, op0, op1):
        reads = [in0, in1]
        s = scalar
        if isinstance(scalar, View):
            reads.append(scalar)
            s = scalar.ap

        def build(e, o=out.ap, a=in0.ap, s=s, b=in1.ap, op0=op0, op1=op1):
            return e.scalar_tensor_tensor(o, a, s, b, op0, op1)
        self.op(eng, build, reads=reads, writes=[out])

    def red(self, eng, out, in_, op, negate=False):
        def build(e, o=out.ap, i=in_.ap, op=op, ng=negate):
            return e.tensor_reduce(o, i, AX.X, op, negate=ng)
        self.op(eng, build, reads=[in_], writes=[out])

    def memset(self, eng, out, val):
        def build(e, o=out.ap, v=val):
            return e.memset(o, v)
        self.op(eng, build, reads=[], writes=[out])

    def recip(self, out, in_):
        def build(e, o=out.ap, i=in_.ap):
            return e.reciprocal(o, i)
        self.op("vector", build, reads=[in_], writes=[out])

    def match_replace(self, out, vals, in_, imm):
        def build(e, o=out.ap, v=vals.ap, i=in_.ap, imm=imm):
            return e.match_replace(o, v, i, imm)
        self.op("vector", build, reads=[vals, in_], writes=[out])

    def sbuf(self, ctx, name, shape, dt):
        self.n_ops += 0
        self._uid = getattr(self, "_uid", 0) + 1
        return Buf(self, ctx.enter_context(
            self.nc.sbuf_tensor(f"{name}_u{self._uid}", shape, dt)), "S")

    def psum(self, ctx, name, shape, dt=F32, side=None):
        self._uid = getattr(self, "_uid", 0) + 1
        return Buf(self, ctx.enter_context(
            self.nc.psum_tensor(f"{name}_u{self._uid}", shape, dt, side=side)), "P")



def make_const_arrays():
    c = {}
    cos = sin = None
    try:
        import jax
        import jax.numpy as jnp
        cpu = jax.devices("cpu")[0]
        with jax.default_device(cpu):
            freqs_j = 1.0 / 500.0 ** (jnp.arange(32, dtype=jnp.float32) / 32)
            ang_j = 1.0 * jnp.arange(T, dtype=jnp.float32)[:, None] * freqs_j[None, :]
            cos = np.asarray(jnp.cos(ang_j), dtype=np.float32).T
            sin = np.asarray(jnp.sin(ang_j), dtype=np.float32).T
    except Exception:
        pass
    if cos is None:
        j = np.arange(32, dtype=np.float64)
        freqs = (1.0 / (500.0 ** (j / 32.0))).astype(np.float32)
        t = np.arange(T, dtype=np.float32)
        ang = (t[None, :] * freqs[:, None]).astype(np.float32)
        ang64 = ang.astype(np.float64)
        cos = np.cos(ang64).astype(np.float32)
        sin = np.sin(ang64).astype(np.float32)
    c["rope_cos"] = np.tile(cos, (4, 1)).astype(np.float32)
    c["rope_sin"] = np.concatenate([-sin, sin, -sin, sin], axis=0).astype(np.float32)
    R = np.zeros((128, 128), dtype=np.float64)
    for blk in range(2):
        o = blk * 64
        for i in range(32):
            R[o + i, o + 32 + i] = 1.0
            R[o + 32 + i, o + i] = 1.0
    c["rperm"] = R.astype(BF)
    c["ident_bf"] = np.eye(128, dtype=BF)
    c["ident_f32"] = np.eye(128, dtype=np.float32)
    s8q = np.zeros((128, 8), dtype=BF)
    s8k = np.zeros((128, 8), dtype=BF)
    s8q[0:64, 0] = 1
    s8q[64:128, 1] = 1
    s8k[0:64, 4] = 1
    s8k[64:128, 5] = 1
    c["stats8q"] = s8q
    c["stats8k"] = s8k
    c["ones_1x64_f32"] = np.ones((1, 64), dtype=np.float32)
    c["ones_1x64_bf"] = np.ones((1, 64), dtype=BF)
    c["negq_1x128"] = np.full((1, 128), -ATT_SCALE / 2.0, dtype=np.float32)
    c["inv1024_1x128"] = np.full((1, 128), 1.0 / 1024.0, dtype=np.float32)
    c["ones_128x1_f32"] = np.ones((128, 1), dtype=np.float32)
    c["ones_1x128_bf"] = np.ones((1, 128), dtype=BF)
    c["eps_128x1"] = np.full((128, 1), EPS, dtype=np.float32)
    sel = np.zeros((8, 8 * 128), dtype=BF)
    for e in range(8):
        sel[e, e * 128:(e + 1) * 128] = 1
    c["sel8"] = sel
    return c


DT_OF = {np.dtype(np.float32): F32, np.dtype(BF): BF16, np.dtype(np.int32): I32}


def load_consts(P, st, ctx, names):
    nc = P.nc
    arrays = st["carrays"]
    for name in names:
        if name in st["c"]:
            continue
        arr = arrays[name]
        h = nc.inline_tensor(np.ascontiguousarray(arr), "c_" + name)
        buf = P.sbuf(ctx, "cs_" + name, list(arr.shape), DT_OF[arr.dtype])
        P.dma("sync", buf.v, DView(h.ap()))
        st["c"][name] = buf


def unload_consts(st, names):
    for name in names:
        st["c"].pop(name, None)



def load_mat_bf(P, ctx, name, dram_ap, R, C, col0=0, cols=None, n_dma=1):
    cols = C if cols is None else cols
    rch = R // 128
    buf = P.sbuf(ctx, name, [128, rch * cols], BF16)
    src = dram_ap[:, col0:col0 + cols] if (col0 or cols != C) else dram_ap
    src = src.rearrange("(r p) c -> p r c", p=128)
    dst = buf.v.with_ap(buf.h.ap().rearrange("p (r c) -> p r c", c=cols))
    step = max(1, rch // n_dma)
    i = 0
    while i < rch:
        j = min(rch, i + step)
        P.dma("gpsimd",
              View(buf, 0, 128, i * cols, j * cols).with_ap(dst.ap[:, i:j, :]),
              DView(src[:, i:j, :], name=dram_ap.tensor.name))
        i = j
    return buf



def emit_load_x(P, st, x_dram, kind):
    nc = P.nc
    xr, xb = st["xr"], st["xb"]
    if kind == "feat":
        P.dma("sync", xr.v, DView(x_dram))
        for cc in range(CH):
            P.act(xb[:, cc * T:(cc + 1) * T], xr[:, cc * T:(cc + 1) * T], AF.Copy)
        return
    with ExitStack() as ctx:
        xtok = P.sbuf(ctx, "xtok", [128, TT * D], F32)
        P.dma("sync",
              xtok.v.with_ap(xtok.h.ap().rearrange("p (t d) -> p t d", d=D)),
              DView(x_dram.rearrange("(t p) d -> p t d", p=128)))
        ps = [P.psum(ctx, f"pstr{i}", [128, 128]) for i in range(2)]
        idf = st["c"]["ident_f32"]
        for tt in range(TT):
            for cc in range(CH):
                p = ps[(tt * CH + cc) % 2]
                P.mm(p.v, xtok[:, tt * D + cc * 128: tt * D + cc * 128 + 128],
                     idf.v)
                P.copy("vector",
                       xr[:, cc * T + tt * 128: cc * T + tt * 128 + 128], p.v)
                P.act(xb[:, cc * T + tt * 128: cc * T + tt * 128 + 128], p.v,
                      AF.Copy)


def emit_store_x(P, st, out_dram, kind):
    nc = P.nc
    xr = st["xr"]
    if kind == "feat":
        P.dma("sync", DView(out_dram), xr.v)
        return
    with ExitStack() as ctx:
        otok = P.sbuf(ctx, "otok", [128, TT * D], F32)
        ps = [P.psum(ctx, f"psot{i}", [128, 128]) for i in range(2)]
        idf = st["c"]["ident_f32"]
        for tt in range(TT):
            for cc in range(CH):
                p = ps[(tt * CH + cc) % 2]
                P.mm(p.v, xr[:, cc * T + tt * 128: cc * T + tt * 128 + 128],
                     idf.v)
                P.copy("vector",
                       otok[:, tt * D + cc * 128: tt * D + cc * 128 + 128], p.v)
        P.dma("sync", DView(out_dram.rearrange("(t p) d -> p t d", p=128)),
              otok.v)


def emit_postln(P, st):
    nc = P.nc
    c = st["c"]
    xr, xb = st["xr"], st["xb"]
    with ExitStack() as ctx:
        sq = [P.sbuf(ctx, f"ln_sq{i}", [128, TC], F32) for i in range(2)]
        st1 = [P.sbuf(ctx, f"ln_s1{i}", [1, TC], F32) for i in range(2)]
        st2 = [P.sbuf(ctx, f"ln_s2{i}", [1, TC], F32) for i in range(2)]
        bcm = P.sbuf(ctx, "ln_bcm", [128, TC], F32)
        bc2 = P.sbuf(ctx, "ln_bc2", [128, TC], F32)
        tmpm = P.sbuf(ctx, "ln_tmpm", [128, TC], F32)
        var = P.sbuf(ctx, "ln_var", [128, TC], F32)
        lnv = P.sbuf(ctx, "ln_lnv", [128, TC], F32)
        rstd = P.sbuf(ctx, "ln_rstd", [128, TC], F32)
        tmp = [P.sbuf(ctx, f"ln_tmp{i}", [128, TC], F32) for i in range(2)]
        ps1 = [P.psum(ctx, f"ln_ps1{i}", [1, TC]) for i in range(2)]
        ps2 = [P.psum(ctx, f"ln_ps2{i}", [1, TC]) for i in range(2)]
        psb = P.psum(ctx, "ln_psb", [128, TC])
        psb2 = P.psum(ctx, "ln_psb2", [128, TC])
        ones = c["ones_128x1_f32"]

        def stats(tc):
            t0 = tc * TC
            p1, p2 = ps1[tc % 2], ps2[tc % 2]
            for cc in range(CH):
                src = xr[:, cc * T + t0: cc * T + t0 + TC]
                P.act(sq[cc % 2].v, src, AF.Square)
                P.mm(p1.v, ones.v, src, start=(cc == 0), stop=(cc == CH - 1))
                P.mm(p2.v, ones.v, sq[cc % 2].v, start=(cc == 0),
                     stop=(cc == CH - 1))

        def chain(tc):
            t0 = tc * TC
            P.act(st1[tc % 2].v, ps1[tc % 2].v, AF.Copy)
            P.act(st2[tc % 2].v, ps2[tc % 2].v, AF.Copy)
            P.mm(psb.v, c["inv1024_1x128"].v, st1[tc % 2].v)
            P.mm(psb2.v, c["inv1024_1x128"].v, st2[tc % 2].v)
            P.act(bcm.v, psb.v, AF.Copy)
            P.act(bc2.v, psb2.v, AF.Copy)
            P.tt("vector", tmpm.v, bcm.v, bcm.v, OP.mult)
            P.tt("vector", var.v, bc2.v, tmpm.v, OP.subtract)
            P.act(lnv.v, var.v, AF.Ln, bias=c["eps_128x1"].v)
            P.act(rstd.v, lnv.v, AF.Exp, scale=-0.5)
            for cc in range(CH):
                xv = xr[:, cc * T + t0: cc * T + t0 + TC]
                eng = "vector" if cc % 2 == 0 else "gpsimd"
                tb = tmp[cc % 2]
                P.tt(eng, tb.v, xv, bcm.v, OP.subtract)
                P.tt(eng, xv, tb.v, rstd.v, OP.mult)
                P.act(xb[:, cc * T + t0: cc * T + t0 + TC], xv, AF.Copy)

        stats(0)
        for tc in range(1, NTC):
            stats(tc)
            chain(tc - 1)
        chain(NTC - 1)


def emit_attn(P, st, layer, W):
    nc = P.nc
    c = st["c"]
    xr, xb = st["xr"], st["xb"]
    rope = (layer == 0)
    with ExitStack() as ctx:
        if rope:
            load_consts(P, st, ctx, ["rope_cos", "rope_sin"])
        if layer == 0:
            bstage = P.sbuf(ctx, "bstage", [24, 128], F32)
            P.dma("sync", bstage.v,
                  DView(W["b_qkv"].rearrange("(a p) -> a p", p=128)))
            bqT = P.sbuf(ctx, "bqT", [128, 24], F32)
            bq_bf = P.sbuf(ctx, "bq_bf", [128, 24], BF16)
            with ExitStack() as bctx:
                psbq = P.psum(bctx, "psbq", [128, 24])
                P.mm(psbq.v, bstage.v, c["ident_f32"][0:24, 0:24])
                P.act(bqT.v, psbq.v, AF.Copy)
            P.act(bq_bf.v, bqT.v, AF.Copy)
            bo_dram = W["b_out"]
        else:
            bo_dram = W["bo"]
        bostage = P.sbuf(ctx, "bostage", [8, 128], F32)
        P.dma("sync", bostage.v,
              DView(bo_dram.rearrange("(a p) -> a p", p=128)))
        boT = P.sbuf(ctx, "boT", [128, 8], F32)
        with ExitStack() as bctx:
            psbo = P.psum(bctx, "psbo", [128, 8])
            P.mm(psbo.v, bostage.v, c["ident_f32"][0:8, 0:8])
            P.act(boT.v, psbo.v, AF.Copy)

        Oall = P.sbuf(ctx, "Oall", [128, CH * T], BF16)
        biasp = [P.sbuf(ctx, f"biasp{i}", [128, 1], F32) for i in range(2)]

        if layer == 0:
            wsrc = {"q": (W["wqkv"], 0), "k": (W["wqkv"], D),
                    "v": (W["wqkv"], 2 * D)}
        else:
            wsrc = {"q": (W["wq"], 0), "k": (W["wk"], 0), "v": (W["wv"], 0)}

        with ExitStack() as pctx:
            wqb = [P.sbuf(pctx, f"wqb{i}", [128, CH * 128], BF16) for i in range(2)]
            wkb = [P.sbuf(pctx, f"wkb{i}", [128, CH * 128], BF16) for i in range(2)]
            wvb = [P.sbuf(pctx, "wvb0", [128, CH * 128], BF16)] * 2
            QT = [P.sbuf(pctx, f"QT{i}", [128, T], BF16) for i in range(2)]
            KT = [P.sbuf(pctx, f"KT{i}", [128, T], BF16) for i in range(2)]
            VP = [P.sbuf(pctx, f"VP{i}", [128, TT * 130 + 64], BF16) for i in range(2)]
            VT = P.sbuf(pctx, "VT", [128, TC], BF16)
            qtmp = P.sbuf(pctx, "qtmp", [128, TC], BF16)
            rtmp = P.sbuf(pctx, "rtmp", [128, TC], BF16)
            sqb = [P.sbuf(pctx, "attsq0", [128, TC], BF16)] * 2
            smax8 = P.sbuf(pctx, "smax8", [8, 8], F32)
            sm1 = P.sbuf(pctx, "sm1", [8, 1], F32)
            srow8 = P.sbuf(pctx, "srow8", [1, 8], F32)
            mqk = P.sbuf(pctx, "mqk", [1, 2], F32)
            mqk1 = P.sbuf(pctx, "mqk1", [1, 1], F32)
            rec_sb = P.sbuf(pctx, "rec_sb", [128, TC], F32)
            Pa = [P.sbuf(pctx, f"Pa{i}", [128, 2 * TC], BF16) for i in range(3)]
            sra = P.sbuf(pctx, "sra", [1, T], BF16)
            srb = P.sbuf(pctx, "srb", [1, T], BF16)
            for i in range(2):
                P.memset("vector", VP[i].v, 1.0)

            def _wload(key, buf, hp):
                dram, off = wsrc[key]
                base = off + hp * 128
                src = dram[:, base:base + 128].rearrange(
                    "(r p) c -> p r c", p=128)
                dst = buf.v.with_ap(
                    buf.h.ap().rearrange("p (r c) -> p r c", c=128))
                P.dma("gpsimd", dst, DView(src, name=dram.tensor.name))

            def load_slice_qk(par, hp):
                _wload("q", wqb[par], hp)
                _wload("k", wkb[par], hp)

            def load_slice_v(hp):
                _wload("v", wvb[0], hp)

            def gen_phase_a(hp):
                par = hp % 2
                wq, wk, wv = wqb[par], wkb[par], wvb[par]
                qt, kt, vp = QT[par], KT[par], VP[par]
                with ExitStack() as actx:
                    psA = [P.psum(actx, f"psA{i}", [128, TC]) for i in range(2)]
                    for tc in range(NTC):
                        t0 = tc * TC
                        for which in (0, 1, 2):
                            wmat = (wq, wk, wv)[which]
                            pq = psA[which % 2]
                            for cc in range(CH):
                                P.mm(pq.v, wmat[:, cc * 128:(cc + 1) * 128],
                                     xb[:, cc * T + t0: cc * T + t0 + TC],
                                     start=(cc == 0), stop=(cc == CH - 1))
                                yield
                            if which == 2:
                                P.copy("vector", VT.v, pq.v)
                                yield
                                for j in range(TC // 128):
                                    tt = tc * (TC // 128) + j
                                    pv = psA[j % 2]
                                    P.mm(View(pv, 0, 128, 0, 128),
                                         VT[:, j * 128:(j + 1) * 128],
                                         c["ident_bf"].v)
                                    yield
                                    P.copy("vector",
                                           View(vp, 0, 128, tt * 130, tt * 130 + 64),
                                           View(pv, 0, 128, 0, 64))
                                    yield
                                    P.copy("vector",
                                           View(vp, 0, 128, tt * 130 + 65, tt * 130 + 129),
                                           View(pv, 0, 128, 64, 128))
                                    yield
                                continue
                            dstT = qt if which == 0 else kt
                            if layer == 0:
                                bia = bqT[:, (8 * which + hp):(8 * which + hp + 1)]
                            else:
                                bia = None
                            ee = "vector" if which == 0 else "gpsimd"
                            e2 = "gpsimd" if which == 0 else "vector"
                            if rope:
                                if bia is not None:
                                    P.ts("vector", qtmp.v, pq.v, bia, OP.add)
                                else:
                                    P.copy("vector", qtmp.v, pq.v)
                                yield
                                P.mm(pq.v, c["rperm"].v, qtmp.v)
                                yield
                                P.tt(ee, dstT[:, t0:t0 + TC], qtmp.v,
                                     c["rope_cos"][:, t0:t0 + TC], OP.mult)
                                yield
                                P.tt("vector", rtmp.v, pq.v,
                                     c["rope_sin"][:, t0:t0 + TC], OP.mult)
                                yield
                                P.tt(ee, dstT[:, t0:t0 + TC],
                                     dstT[:, t0:t0 + TC], rtmp.v, OP.add)
                                yield
                            else:
                                if bia is not None:
                                    P.ts("vector", dstT[:, t0:t0 + TC], pq.v,
                                         bia, OP.add)
                                else:
                                    P.copy("vector", dstT[:, t0:t0 + TC],
                                           pq.v)
                                yield
                            sq = sqb[which]
                            P.tt(e2, sq.v, dstT[:, t0:t0 + TC],
                                 dstT[:, t0:t0 + TC], OP.mult)
                            yield
                            P.mm(View(psA[which % 2], 0, 8, 0, TC),
                                 c["stats8q" if which == 0 else "stats8k"].v,
                                 sq.v)
                            yield
                            P.red("vector",
                                  smax8[:, (4 * which + tc):(4 * which + tc + 1)],
                                  View(psA[which % 2], 0, 8, 0, TC), OP.max)
                            yield
                    P.red("vector", sm1.v, smax8[:, 0:8], OP.max)
                    yield
                    P.mm(View(psA[0], 0, 1, 0, 8), sm1.v, c["ident_f32"][0:8, 0:8])
                    yield
                    P.copy("vector", srow8.v, View(psA[0], 0, 1, 0, 8))
                    yield
                    P.tt("vector", mqk.v, srow8[:, 0:2], srow8[:, 4:6], OP.add)
                    yield
                    P.red("vector", mqk1.v, mqk.v, OP.max)
                    yield
                    P.mm(View(psA[1], 0, 128, 0, 1), c["negq_1x128"].v, mqk1.v)
                    yield
                    P.copy("vector", biasp[par].v, View(psA[1], 0, 128, 0, 1))
                    yield

            def run_core(hp, filler):
                par = hp % 2
                qt, kt, vp = QT[par], KT[par], VP[par]

                def step(n_pe=2, cap=8):
                    if filler is None:
                        return
                    pe0 = P.cnt["Et"]
                    for _ in range(cap):
                        if next(filler, "END") == "END":
                            break
                        if P.cnt["Et"] - pe0 >= n_pe:
                            break
                with ExitStack() as cctx:
                    psOa = P.psum(cctx, "psOa", [128, TC])
                    psOb = P.psum(cctx, "psOb", [128, TC])
                    psS = [P.psum(cctx, f"psS{i}", [128, 2 * TC]) for i in range(2)]
                    for tc in range(NTC):
                        t0 = tc * TC

                        def s_mm(kc):
                            k0 = kc * 128
                            ps = psS[kc % 2]
                            P.mm(View(ps, 0, 128, 0, TC),
                                 View(kt, 0, 64, k0, k0 + 128),
                                 View(qt, 0, 64, t0, t0 + TC))
                            P.mm(View(ps, 0, 128, TC, 2 * TC),
                                 View(kt, 64, 128, k0, k0 + 128),
                                 View(qt, 64, 128, t0, t0 + TC))
                        s_mm(0)
                        P.act(Pa[0].v, psS[0].v, AF.Exp,
                              bias=biasp[par][:, 0:1], scale=ATT_SCALE)
                        s_mm(1)
                        for kc in range(TT):
                            if kc + 1 < TT:
                                P.act(Pa[(kc + 1) % 3].v, psS[(kc + 1) % 2].v,
                                      AF.Exp, bias=biasp[par][:, 0:1],
                                      scale=ATT_SCALE)
                            if kc + 2 < TT:
                                s_mm(kc + 2)
                            step(2)
                            pa = Pa[kc % 3]
                            P.mm(psOa.v,
                                 View(vp, 0, 128, kc * 130, kc * 130 + 128),
                                 View(pa, 0, 128, 0, TC),
                                 start=(kc == 0), stop=(kc == TT - 1))
                            P.mm(psOb.v,
                                 View(vp, 0, 128, kc * 130 + 65, kc * 130 + 193),
                                 View(pa, 0, 128, TC, 2 * TC),
                                 start=(kc == 0), stop=(kc == TT - 1))
                        ob = hp * T + t0
                        P.copy("vector", View(Oall, 0, 64, ob, ob + TC),
                               View(psOa, 0, 64, 0, TC))
                        P.copy("vector", sra[:, t0:t0 + TC],
                               View(psOa, 64, 65, 0, TC))
                        P.copy("vector", View(Oall, 64, 128, ob, ob + TC),
                               View(psOb, 0, 64, 0, TC))
                        P.copy("vector", srb[:, t0:t0 + TC],
                               View(psOb, 64, 65, 0, TC))
                        step(4)
                if filler is not None:
                    for _ in filler:
                        pass
                with ExitStack() as dctx:
                    psBc = P.psum(dctx, "psBc", [128, TC], side="right")
                    for tc in range(NTC):
                        t0 = tc * TC
                        ob = hp * T + t0
                        P.mm(View(psBc, 0, 64, 0, TC),
                             c["ones_1x64_bf"].v, sra[:, t0:t0 + TC])
                        P.mm(View(psBc, 64, 128, 0, TC),
                             c["ones_1x64_bf"].v, srb[:, t0:t0 + TC])
                        P.recip(rec_sb.v, psBc.v)
                        for p0 in (0, 64):
                            P.tt("vector", View(Oall, p0, p0 + 64, ob, ob + TC),
                                 View(Oall, p0, p0 + 64, ob, ob + TC),
                                 View(rec_sb, p0, p0 + 64, 0, TC), OP.mult)

            load_slice_qk(0, 0)
            load_slice_qk(1, 1)
            load_slice_v(0)
            g = gen_phase_a(0)
            for _ in g:
                pass
            for hp in range(8):
                if hp + 2 < 8:
                    load_slice_qk(hp % 2, hp + 2)
                if hp + 1 < 8:
                    load_slice_v(hp + 1)
                filler = gen_phase_a(hp + 1) if hp + 1 < 8 else None
                run_core(hp, filler)

        wout = load_mat_bf(P, ctx, "woutb", W["wout" if layer == 0 else "wo"],
                           D, D, n_dma=2)
        if layer == 0:
            beff = P.sbuf(ctx, "beff", [128, 8], F32)
            with ExitStack() as bctx:
                psbe = P.psum(bctx, "psbe", [128, 8])
                for f in range(8):
                    for cc in range(CH):
                        P.mm(View(psbe, 0, 128, f, f + 1),
                             wout[:, cc * D + f * 128: cc * D + (f + 1) * 128],
                             bq_bf[:, 16 + cc:17 + cc],
                             start=(cc == 0), stop=(cc == CH - 1))
                P.tt("vector", beff.v, View(psbe, 0, 128, 0, 8), boT.v, OP.add)
        else:
            beff = boT
        with ExitStack() as octx:
            psY = [P.psum(octx, f"psY{i}", [128, TC]) for i in range(2)]
            for tc in range(NTC):
                t0 = tc * TC
                for f in range(8):
                    py = psY[f % 2]
                    for cc in range(CH):
                        P.mm(py.v, wout[:, cc * D + f * 128: cc * D + (f + 1) * 128],
                             View(Oall, 0, 128, cc * T + t0, cc * T + t0 + TC),
                             start=(cc == 0), stop=(cc == CH - 1))
                    P.stt("vector", xr[:, f * T + t0: f * T + t0 + TC],
                          py.v, beff[:, f:f + 1],
                          xr[:, f * T + t0: f * T + t0 + TC], OP.add, OP.add)
    if rope:
        unload_consts(st, ["rope_cos", "rope_sin"])


def emit_ffn(P, st, W):
    nc = P.nc
    c = st["c"]
    xr, xb = st["xr"], st["xb"]
    DEQ2 = (1.0 / 127.0) ** 2
    HH = HID // 2
    NHC = HH // 128
    for half in range(2):
        with ExitStack() as ctx:
            wgT = P.sbuf(ctx, "wgT", [128, CH * HH], BF16)
            wuT = P.sbuf(ctx, "wuT", [128, CH * HH], BF16)
            wdT = P.sbuf(ctx, "wdT", [128, NHC * D], BF16)
            hbuf = P.sbuf(ctx, "hbuf", [128, NHC * TC], BF16)
            gs = [P.sbuf(ctx, f"gs{i}", [128, TC], BF16) for i in range(2)]
            for dst, scr in ((wgT, st["scr_wg"]), (wuT, st["scr_wu"])):
                for cc in range(CH):
                    P.dma("sync",
                          dst[:, cc * HH:(cc + 1) * HH],
                          DView(scr.ap()[half * HH:(half + 1) * HH,
                                         cc * 128:(cc + 1) * 128],
                                name=scr.name),
                          transpose=True)
            for hc in range(NHC):
                P.dma("sync",
                      wdT[:, hc * D:(hc + 1) * D],
                      DView(st["scr_wd"].ap()[:, half * HH + hc * 128:
                                              half * HH + (hc + 1) * 128],
                            name=st["scr_wd"].name),
                      transpose=True)
            with ExitStack() as cctx:
                psG = [P.psum(cctx, f"psG{i}", [128, TC]) for i in range(2)]
                psU = [P.psum(cctx, f"psU{i}", [128, TC]) for i in range(2)]
                psY = [P.psum(cctx, f"psYf{i}", [128, TC]) for i in range(2)]
                for tc in range(NTC):
                    t0 = tc * TC
                    for hc in range(NHC):
                        pg, pu = psG[hc % 2], psU[hc % 2]
                        for cc in range(CH):
                            P.mm(pg.v, wgT[:, cc * HH + hc * 128: cc * HH + (hc + 1) * 128],
                                 xb[:, cc * T + t0: cc * T + t0 + TC],
                                 start=(cc == 0), stop=(cc == CH - 1))
                        for cc in range(CH):
                            P.mm(pu.v, wuT[:, cc * HH + hc * 128: cc * HH + (hc + 1) * 128],
                                 xb[:, cc * T + t0: cc * T + t0 + TC],
                                 start=(cc == 0), stop=(cc == CH - 1))
                        g = gs[hc % 2]
                        P.act(g.v, pg.v, AF.Silu, scale=1.0 / 127.0)
                        P.tt("vector", hbuf[:, hc * TC:(hc + 1) * TC],
                             g.v, pu.v, OP.mult)
                    for f in range(CH):
                        py = psY[f % 2]
                        for hc in range(NHC):
                            P.mm(py.v, wdT[:, hc * D + f * 128: hc * D + (f + 1) * 128],
                                 hbuf[:, hc * TC:(hc + 1) * TC],
                                 start=(hc == 0), stop=(hc == NHC - 1))
                        P.stt("vector", xr[:, f * T + t0: f * T + t0 + TC],
                              py.v, DEQ2, xr[:, f * T + t0: f * T + t0 + TC],
                              OP.mult, OP.add)


def emit_moe(P, st, W):
    nc = P.nc
    c = st["c"]
    xr, xb = st["xr"], st["xb"]
    with ExitStack() as ctx:
        gstage = P.sbuf(ctx, "mgstage", [8, D], F32)
        P.dma("sync", gstage.v, DView(W["gate_w"]))
        gwT = P.sbuf(ctx, "gwT", [128, CH * 8], BF16)
        gbstage = P.sbuf(ctx, "gbstage", [1, 8], F32)
        P.dma("sync", gbstage.v, DView(W["gate_b"][None, :], name="gate_b"))
        with ExitStack() as tctx:
            pst = P.psum(tctx, "mpst", [128, 8])
            for cc in range(CH):
                P.mm(pst.v, gstage[0:8, cc * 128:(cc + 1) * 128],
                     c["ident_f32"][0:8, 0:8])
                P.copy("vector", gwT[:, cc * 8:(cc + 1) * 8], pst.v)
            psgb = P.psum(tctx, "psgb", [8, 1])
            P.mm(psgb.v, gbstage.v, c["ident_f32"][0:1, 0:1])
            gbT = P.sbuf(ctx, "gbT", [8, 1], F32)
            P.act(gbT.v, psgb.v, AF.Copy)
        eb = P.sbuf(ctx, "ebb", [8, D], BF16)
        P.dma("gpsimd", eb.v, DView(W["expert_b"]))
        wT = P.sbuf(ctx, "wTmoe", [8, T], BF16)
        with ExitStack() as gctx:
            psL = P.psum(gctx, "psL", [8, TC])
            psG2 = P.psum(gctx, "psG2", [128, 8])
            psW2 = P.psum(gctx, "psW2", [8, 128])
            Lsb = P.sbuf(gctx, "Lsb", [8, TC], F32)
            gt = P.sbuf(gctx, "gt", [128, 8], F32)
            negm = P.sbuf(gctx, "negm", [128, 1], F32)
            ex = P.sbuf(gctx, "ex", [128, 8], F32)
            sume = P.sbuf(gctx, "sume", [128, 1], F32)
            rec = P.sbuf(gctx, "rec", [128, 1], F32)
            gg = P.sbuf(gctx, "gg", [128, 8], F32)
            m1 = P.sbuf(gctx, "m1", [128, 8], F32)
            gm = P.sbuf(gctx, "gm", [128, 8], F32)
            m2 = P.sbuf(gctx, "m2", [128, 1], F32)
            msk = P.sbuf(gctx, "msk", [128, 8], F32)
            wtk = P.sbuf(gctx, "wtk", [128, 8], BF16)
            for tc in range(NTC):
                t0 = tc * TC
                for cc in range(CH):
                    P.mm(psL.v, gwT[:, cc * 8:(cc + 1) * 8],
                         xb[:, cc * T + t0: cc * T + t0 + TC],
                         start=(cc == 0), stop=(cc == CH - 1))
                P.act(Lsb.v, psL.v, AF.Copy, bias=gbT.v)
                for ttl in range(TC // 128):
                    P.mm(psG2.v, Lsb[0:8, ttl * 128:(ttl + 1) * 128],
                         c["ident_f32"][0:8, 0:8])
                    P.copy("vector", gt.v, psG2.v)
                    P.red("vector", negm.v, gt.v, OP.max, negate=True)
                    P.act(ex.v, gt.v, AF.Exp, bias=negm.v, accum=sume.v)
                    P.recip(rec.v, sume.v)
                    P.ts("vector", gg.v, ex.v, rec.v, OP.mult)
                    P.memset("vector", m1.v, -1e30)
                    P.red("vector", m1[:, 0:1], gg.v, OP.max)
                    P.match_replace(gm.v, m1.v, gg.v, -1e30)
                    P.red("vector", m2.v, gm.v, OP.max)
                    P.ts("vector", msk.v, gg.v, m2.v, OP.is_ge)
                    P.tt("vector", wtk.v, gg.v, msk.v, OP.mult)
                    P.mm(psW2.v, wtk.v, c["ident_bf"].v)
                    P.copy("vector",
                           wT[0:8, t0 + ttl * 128: t0 + (ttl + 1) * 128],
                           View(psW2, 0, 8, 0, 128))
        ewT = [P.sbuf(ctx, f"ewT{i}", [128, CH * D], BF16) for i in range(4)]
        bca = P.sbuf(ctx, "bca", [128, 2 * T], BF16)
        xe = P.sbuf(ctx, "xe", [128, 2 * CH * TC], BF16)
        scr_ew = st["scr_ew"]

        def load_pair(pg):
            for i, e in enumerate((2 * pg, 2 * pg + 1)):
                slot = (2 * pg + i) % 4
                for cc in range(CH):
                    P.dma("sync",
                          ewT[slot][:, cc * D:(cc + 1) * D],
                          DView(scr_ew.ap()[e * D:(e + 1) * D,
                                            cc * 128:(cc + 1) * 128],
                                name=scr_ew.name),
                          transpose=True)

        load_pair(0)
        for pg in range(4):
            e0, e1 = 2 * pg, 2 * pg + 1
            if pg < 3:
                load_pair(pg + 1)
            with ExitStack() as bctx:
                psBc = P.psum(bctx, "psBcM", [128, TC])
                for i, e in enumerate((e0, e1)):
                    for tc in range(NTC):
                        t0 = tc * TC
                        P.mm(psBc.v, c["sel8"][0:8, e * 128:(e + 1) * 128],
                             wT[0:8, t0:t0 + TC])
                        P.act(bca[:, i * T + t0: i * T + t0 + TC], psBc.v,
                              AF.Copy)
            with ExitStack() as mctx:
                psM = [P.psum(mctx, f"psM{i}", [128, TC]) for i in range(CH)]
                for tc in range(NTC):
                    t0 = tc * TC
                    for i in range(2):
                        for cc in range(CH):
                            P.tt("vector",
                                 xe[:, (i * CH + cc) * TC:(i * CH + cc + 1) * TC],
                                 xb[:, cc * T + t0: cc * T + t0 + TC],
                                 bca[:, i * T + t0: i * T + t0 + TC], OP.mult)
                        ew = ewT[(2 * pg + i) % 4]
                        for f in range(CH):
                            for cc in range(CH):
                                P.mm(psM[f].v,
                                     ew[:, cc * D + f * 128: cc * D + (f + 1) * 128],
                                     xe[:, (i * CH + cc) * TC:(i * CH + cc + 1) * TC],
                                     start=(i == 0 and cc == 0),
                                     stop=(pg < 3 and i == 1 and cc == CH - 1))
                    for f in range(CH):
                        if pg == 3:
                            P.mm(psM[f].v, eb[0:8, f * 128:(f + 1) * 128],
                                 wT[0:8, t0:t0 + TC], start=False, stop=True)
                        P.tt("vector", xr[:, f * T + t0: f * T + t0 + TC],
                             psM[f].v, xr[:, f * T + t0: f * T + t0 + TC],
                             OP.add)



STAGE_CONSTS = {
    "l0_attn": ["ident_f32", "ident_bf", "rperm",
                "stats8q", "stats8k", "ones_1x64_f32", "negq_1x128", "ones_1x64_bf"],
    "l0_ffn": ["ident_bf", "inv1024_1x128", "ones_128x1_f32"],
    "l1_attn": ["ident_f32", "ident_bf", "stats8q", "stats8k", "ones_1x64_bf",
                "ones_1x64_f32", "negq_1x128"],
    "l1_moe": ["ident_f32", "ident_bf", "ones_1x128_bf", "sel8"],
    "ln": ["inv1024_1x128", "ones_128x1_f32", "eps_128x1"],
    "io": ["ident_f32"],
}


def build_program(stages, in_kind, out_kind):
    nc = bass.Bass("TRN2", target_bir_lowering=False)
    need = set(STAGE_CONSTS["io"]) | set(STAGE_CONSTS["ln"])
    for s in stages:
        need |= set(STAGE_CONSTS[s])
    ins = {}
    if in_kind == "tok":
        x_in = nc.dram_tensor("x_in", [T, D], F32, kind="ExternalInput").ap()
        ins["x_in"] = "x_tok"
    else:
        x_in = nc.dram_tensor("h_in", [128, CH * T], F32,
                              kind="ExternalInput").ap()
        ins["h_in"] = "handoff"
    W = {}
    def din(name, shape, dt=F32):
        W[name] = nc.dram_tensor(name, shape, dt, kind="ExternalInput").ap()
        ins[name] = "w"
    if "l0_attn" in stages:
        din("wqkv", [D, 3 * D])
        din("b_qkv", [3 * D])
        din("wout", [D, D])
        din("b_out", [D])
    if "l0_ffn" in stages:
        din("w_gate", [HID, D], I32)
        din("w_up", [HID, D], I32)
        din("w_down", [D, HID], I32)
    if "l1_attn" in stages:
        din("wq", [D, D])
        din("wk", [D, D])
        din("wv", [D, D])
        din("wo", [D, D])
        din("bo", [D])
    if "l1_moe" in stages:
        din("gate_w", [NE, D])
        din("gate_b", [NE])
        din("expert_w", [NE, D, D])
        din("expert_b", [NE, D])
    if out_kind == "tok":
        x_out = nc.dram_tensor("x_out", [T, D], F32, kind="ExternalOutput").ap()
    else:
        x_out = nc.dram_tensor("h_out", [128, CH * T], F32,
                               kind="ExternalOutput").ap()

    with ExitStack() as ctx:
        P = Prog(nc, ctx)
        st = {"carrays": make_const_arrays(), "c": {}}
        st["xr"] = P.sbuf(ctx, "xr", [128, CH * T], F32)
        st["xb"] = P.sbuf(ctx, "xb", [128, CH * T], BF16)
        if "l0_ffn" in stages:
            st["scr_wg"] = nc.dram_tensor("scr_wg", [HID, D], BF16, kind="Internal")
            st["scr_wu"] = nc.dram_tensor("scr_wu", [HID, D], BF16, kind="Internal")
            st["scr_wd"] = nc.dram_tensor("scr_wd", [D, HID], BF16, kind="Internal")
            for scr, dram in ((st["scr_wg"], W["w_gate"]),
                              (st["scr_wu"], W["w_up"]),
                              (st["scr_wd"], W["w_down"])):
                P.dma("gpsimd", DView(scr.ap(), name=scr.name), DView(dram))
        if "l1_moe" in stages:
            st["scr_ew"] = nc.dram_tensor("scr_ew", [NE * D, D], BF16,
                                          kind="Internal")
            P.dma("gpsimd",
                  DView(st["scr_ew"].ap(), name=st["scr_ew"].name),
                  DView(W["expert_w"].rearrange("e o h -> (e o) h"),
                        name=W["expert_w"].tensor.name))
        load_consts(P, st, ctx, sorted(need))
        emit_load_x(P, st, x_in, in_kind)
        for s in stages:
            if s == "l0_attn":
                emit_attn(P, st, 0, W)
            elif s == "l0_ffn":
                emit_ffn(P, st, W)
            elif s == "l1_attn":
                emit_attn(P, st, 1, W)
            elif s == "l1_moe":
                emit_moe(P, st, W)
            emit_postln(P, st)
        emit_store_x(P, st, x_out, out_kind)
        P.barrier_all()
        P.realize()
    return nc, ins



STAGE_GROUPS = [["l0_attn", "l0_ffn", "l1_attn", "l1_moe"]]

W_INPUT_MAP = {
    "wqkv": "L0_attn_w_qkv", "b_qkv": "L0_attn_b_qkv",
    "wout": "L0_attn_w_out", "b_out": "L0_attn_b_out",
    "w_gate": "L0_ffn_w_gate", "w_up": "L0_ffn_w_up", "w_down": "L0_ffn_w_down",
    "wq": "L1_attn_Wq", "wk": "L1_attn_Wk", "wv": "L1_attn_Wv",
    "wo": "L1_attn_Wo", "bo": "L1_attn_bo",
    "gate_w": "L1_ffn_gate_w", "gate_b": "L1_ffn_gate_b",
    "expert_w": "L1_ffn_expert_w", "expert_b": "L1_ffn_expert_b",
}


def kernel(**inputs):
    x = np.ascontiguousarray(inputs["L0_attn_x"], dtype=np.float32)
    cores = list(range(8))
    stage_groups = STAGE_GROUPS
    cur = [x[b] for b in range(B)]
    n_groups = len(stage_groups)
    for gi, stages in enumerate(stage_groups):
        in_kind = "tok" if gi == 0 else "feat"
        out_kind = "tok" if gi == n_groups - 1 else "feat"
        nc, ins = build_program(stages, in_kind, out_kind)
        in_maps = []
        for b in range(B):
            m = {}
            for name, kind in ins.items():
                if kind == "x_tok" or kind == "handoff":
                    m[name] = np.ascontiguousarray(cur[b])
                else:
                    m[name] = np.ascontiguousarray(
                        inputs[W_INPUT_MAP[name]])
            in_maps.append(m)
        res = run_bass_kernel_spmd(nc, in_maps, core_ids=cores)
        okey = "x_out" if out_kind == "tok" else "h_out"
        cur = [res.results[b][okey] for b in range(B)]
    out = np.stack(cur, axis=0).astype(np.float32)
    return out


# version 18
# speedup vs baseline: 1.0259x; 1.0259x over previous
import math
from contextlib import ExitStack

import numpy as np
import ml_dtypes

import concourse.bass as bass
import concourse.mybir as mybir
from concourse.bass_utils import run_bass_kernel_spmd

F32 = mybir.dt.float32
BF16 = mybir.dt.bfloat16
I32 = mybir.dt.int32
AF = mybir.ActivationFunctionType
OP = mybir.AluOpType
AX = mybir.AxisListType

BF = ml_dtypes.bfloat16

B, T, D = 8, 2048, 1024
H, DH = 16, 64
HID = 3072
NE = 8
CH = D // 128
TT = T // 128
TC = 512
NTC = T // TC
ATT_SCALE = 0.125
EPS = 1e-5

ENGINES = ("tensor", "scalar", "vector", "gpsimd", "sync")
DMA_Q = {"sync": "Dsy", "gpsimd": "Dgp"}
NDMASEM = 8



class Reg:
    __slots__ = ("space", "p0", "p1", "a0", "a1")

    def __init__(self, space, p0, p1, a0, a1):
        self.space, self.p0, self.p1, self.a0, self.a1 = space, p0, p1, a0, a1

    def overlaps(self, o):
        return (self.space == o.space and self.p0 < o.p1 and o.p0 < self.p1
                and self.a0 < o.a1 and o.a0 < self.a1)

    def covers(self, o):
        return (self.space == o.space and self.p0 <= o.p0 and self.p1 >= o.p1
                and self.a0 <= o.a0 and self.a1 >= o.a1)


class Rec:
    __slots__ = ("kind", "reg", "eng", "token", "clock")

    def __init__(self, kind, reg, eng, token, clock):
        self.kind, self.reg, self.eng, self.token, self.clock = (
            kind, reg, eng, token, clock)


class Buf:

    def __init__(self, prog, handle, space):
        self.prog = prog
        self.h = handle
        self.space = space
        self.P = handle.shape[0]
        self.F = int(np.prod(handle.shape[1:]))
        self.esz = mybir.dt.size(handle.dtype)
        mloc = prog.nc.lookup_mloc(handle)
        if space == "P":
            self.addr = int(mloc.bank) * 2048
        else:
            a = getattr(mloc, "addr", None)
            self.addr = a if a is not None else id(handle)

    def __getitem__(self, idx):
        ps, fs = idx
        if isinstance(ps, slice):
            p0, p1 = ps.start or 0, (ps.stop if ps.stop is not None else self.P)
        else:
            p0, p1 = ps, ps + 1
        if isinstance(fs, slice):
            f0, f1 = fs.start or 0, (fs.stop if fs.stop is not None else self.F)
        else:
            f0, f1 = fs, fs + 1
        return View(self, p0, p1, f0, f1)

    @property
    def v(self):
        return self[:, :]


class View:
    __slots__ = ("buf", "p0", "p1", "f0", "f1", "_apo")

    def __init__(self, buf, p0, p1, f0, f1, apo=None):
        self.buf, self.p0, self.p1, self.f0, self.f1 = buf, p0, p1, f0, f1
        self._apo = apo

    def with_ap(self, ap):
        return View(self.buf, self.p0, self.p1, self.f0, self.f1, apo=ap)

    @property
    def ap(self):
        if self._apo is not None:
            return self._apo
        h = self.buf.h
        full = h.ap()
        if len(h.shape) > 2:
            names = " ".join(f"a{i}" for i in range(1, len(h.shape)))
            full = full.rearrange(f"a0 {names} -> a0 ({names})")
        return full[self.p0:self.p1, self.f0:self.f1]

    def reg(self):
        b = self.buf
        if b.space == "P":
            return Reg("P", 0, 128, b.addr, b.addr + b.F * b.esz)
        return Reg(b.space, self.p0, self.p1,
                   b.addr + self.f0 * b.esz, b.addr + self.f1 * b.esz)


class DView:
    __slots__ = ("ap", "_reg")

    def __init__(self, ap, name=None):
        self.ap = ap
        nm = name or ap.tensor.name
        self._reg = Reg("D:" + nm, 0, 128, 0, 1 << 60)

    def reg(self):
        return self._reg


class Prog:
    def __init__(self, nc, ctx):
        self.nc = nc
        self.ctx = ctx
        self.steps = {e: [] for e in ENGINES}
        self.sem = {}
        self.dma_sems = set()
        for nm in ["Et", "Es", "Ev", "Eg"]:
            self.sem[nm] = ctx.enter_context(nc.semaphore(nm))
        for q in DMA_Q.values():
            for k in range(NDMASEM):
                nm = f"{q}{k}"
                self.sem[nm] = ctx.enter_context(nc.semaphore(nm))
                self.dma_sems.add(nm)
        self.esem = {"tensor": "Et", "scalar": "Es", "vector": "Ev",
                     "gpsimd": "Eg"}
        self.cnt = {nm: 0 for nm in self.sem}
        self.dma_rr = {q: 0 for q in DMA_Q.values()}
        self.dma_last = {}
        self.clock = {e: {} for e in ENGINES}
        self.recs = []
        self.n_ops = 0

    def _emit(self, eng, build, reads, writes, token_sem, token_val, mm=False,
              extra_deps=()):
        regs_r = [v.reg() for v in reads if v is not None]
        regs_w = [v.reg() for v in writes if v is not None]
        deps = list(extra_deps)
        for rec in self.recs:
            if rec.kind == "W":
                hit = any(rec.reg.overlaps(r) for r in regs_r) or \
                    any(rec.reg.overlaps(r) for r in regs_w)
            else:
                hit = any(rec.reg.overlaps(r) for r in regs_w)
                if not hit and rec.reg.space == "P" and rec.eng != eng:
                    hit = any(rec.reg.overlaps(r) for r in regs_r)
            if hit:
                deps.append(rec)
        clk = self.clock[eng]
        waits = []
        for rec in deps:
            if mm and rec.eng == "tensor" and eng == "tensor":
                continue
            s, v = rec.token
            if clk.get(s, 0) >= v:
                continue
            waits.append((s, v))
            for cs, cv in rec.clock.items():
                if clk.get(cs, 0) < cv:
                    clk[cs] = cv
            clk[s] = max(clk.get(s, 0), v)
        self.steps[eng].append((waits, build, (token_sem, token_val)))
        snap = dict(clk)
        snap[token_sem] = token_val
        tok = (token_sem, token_val)
        new_recs = []
        for rec in self.recs:
            if any(w.covers(rec.reg) for w in regs_w):
                continue
            new_recs.append(rec)
        me = "tensor" if eng == "tensor" else eng
        for r in regs_w:
            new_recs.append(Rec("W", r, me, tok, snap))
        for r in regs_r:
            new_recs.append(Rec("R", r, me, tok, snap))
        self.recs = new_recs
        self.n_ops += 1

    def op(self, eng, build, reads=(), writes=(), mm=False):
        sname = self.esem[eng]
        self.cnt[sname] += 1
        self._emit(eng, build, reads, writes, sname, self.cnt[sname], mm=mm)

    def dma(self, eng, out, in_, **kw):
        q = DMA_Q[eng]
        slot = self.dma_rr[q]
        self.dma_rr[q] = (slot + 1) % NDMASEM
        sname = f"{q}{slot}"
        self.cnt[sname] += 16
        val = self.cnt[sname]
        out_ap = out.ap if isinstance(out, (View, DView)) else out
        in_ap = in_.ap if isinstance(in_, (View, DView)) else in_

        def build(e, _o=out_ap, _i=in_ap, _kw=kw):
            return e.dma_start(out=_o, in_=_i, **_kw)
        reads = [in_] if isinstance(in_, (View, DView)) else []
        writes = [out] if isinstance(out, (View, DView)) else []
        extra = []
        if sname in self.dma_last:
            pv, pclk = self.dma_last[sname]
            extra.append(Rec("W", Reg("NONE", 0, 0, 0, 0), "dma",
                             (sname, pv), pclk))
        self._emit(eng, build, reads, writes, sname, val, extra_deps=extra)
        snap = dict(self.clock[eng])
        snap[sname] = val
        self.dma_last[sname] = (val, snap)

    def barrier_all(self, eng="sync"):
        waits = []
        for s in self.dma_sems:
            if self.cnt[s]:
                waits.append((s, self.cnt[s]))
        for s in ("Et", "Es", "Ev", "Eg"):
            if self.cnt[s]:
                waits.append((s, self.cnt[s]))
        self.steps[eng].append((waits, None, None))

    def realize(self):
        nc = self.nc
        dmaq = self.dma_sems
        with nc.Block() as block:
            for eng in ENGINES:
                steps = self.steps[eng]
                if not steps:
                    continue

                def runner(e, _steps=steps):
                    for waits, build, tok in _steps:
                        for s, v in waits:
                            e.wait_ge(self.sem[s], v)
                        if build is not None:
                            ins = build(e)
                            if tok is not None:
                                ins.then_inc(
                                    self.sem[tok[0]],
                                    16 if tok[0] in dmaq else 1)
                getattr(block, eng)(runner)

    def mm(self, out, lhsT, rhs, start=True, stop=True):
        def build(e, o=out.ap, l=lhsT.ap, r=rhs.ap, st=start, sp=stop):
            return e.matmul(o, l, r, start=st, stop=sp)
        self.op("tensor", build, reads=[lhsT, rhs], writes=[out], mm=True)

    def act(self, out, in_, func, bias=0.0, scale=1.0, accum=None):
        reads = [in_]
        b, s = bias, scale
        if isinstance(bias, View):
            reads.append(bias)
            b = bias.ap
            if func == AF.Copy:
                func = AF.Identity
        if isinstance(scale, View):
            reads.append(scale)
            s = scale.ap
        writes = [out] + ([accum] if accum is not None else [])

        def build(e, o=out.ap, i=in_.ap, f=func, b=b, s=s,
                  a=(accum.ap if accum is not None else None)):
            return e.activation(o, i, f, bias=b, scale=s, accum_out=a)
        self.op("scalar", build, reads=reads, writes=writes)

    def copy(self, eng, out, in_):
        def build(e, o=out.ap, i=in_.ap):
            return e.tensor_copy(o, i)
        self.op(eng, build, reads=[in_], writes=[out])

    def tt(self, eng, out, a, b, op):
        def build(e, o=out.ap, x=a.ap, y=b.ap, op=op):
            return e.tensor_tensor(o, x, y, op)
        self.op(eng, build, reads=[a, b], writes=[out])

    def ts(self, eng, out, a, scalar, op, accum=None):
        reads = [a]
        s = scalar
        if isinstance(scalar, View):
            reads.append(scalar)
            s = scalar.ap
        writes = [out] + ([accum] if accum is not None else [])

        def build(e, o=out.ap, x=a.ap, s=s, op=op,
                  ac=(accum.ap if accum is not None else None)):
            return e.tensor_scalar(o, x, s, None, op, accum_out=ac)
        self.op(eng, build, reads=reads, writes=writes)

    def stt(self, eng, out, in0, scalar, in1, op0, op1):
        reads = [in0, in1]
        s = scalar
        if isinstance(scalar, View):
            reads.append(scalar)
            s = scalar.ap

        def build(e, o=out.ap, a=in0.ap, s=s, b=in1.ap, op0=op0, op1=op1):
            return e.scalar_tensor_tensor(o, a, s, b, op0, op1)
        self.op(eng, build, reads=reads, writes=[out])

    def red(self, eng, out, in_, op, negate=False):
        def build(e, o=out.ap, i=in_.ap, op=op, ng=negate):
            return e.tensor_reduce(o, i, AX.X, op, negate=ng)
        self.op(eng, build, reads=[in_], writes=[out])

    def memset(self, eng, out, val):
        def build(e, o=out.ap, v=val):
            return e.memset(o, v)
        self.op(eng, build, reads=[], writes=[out])

    def recip(self, out, in_):
        def build(e, o=out.ap, i=in_.ap):
            return e.reciprocal(o, i)
        self.op("vector", build, reads=[in_], writes=[out])

    def match_replace(self, out, vals, in_, imm):
        def build(e, o=out.ap, v=vals.ap, i=in_.ap, imm=imm):
            return e.match_replace(o, v, i, imm)
        self.op("vector", build, reads=[vals, in_], writes=[out])

    def sbuf(self, ctx, name, shape, dt):
        self.n_ops += 0
        self._uid = getattr(self, "_uid", 0) + 1
        return Buf(self, ctx.enter_context(
            self.nc.sbuf_tensor(f"{name}_u{self._uid}", shape, dt)), "S")

    def psum(self, ctx, name, shape, dt=F32, side=None):
        self._uid = getattr(self, "_uid", 0) + 1
        return Buf(self, ctx.enter_context(
            self.nc.psum_tensor(f"{name}_u{self._uid}", shape, dt, side=side)), "P")



def make_const_arrays():
    c = {}
    cos = sin = None
    try:
        import jax
        import jax.numpy as jnp
        cpu = jax.devices("cpu")[0]
        with jax.default_device(cpu):
            freqs_j = 1.0 / 500.0 ** (jnp.arange(32, dtype=jnp.float32) / 32)
            ang_j = 1.0 * jnp.arange(T, dtype=jnp.float32)[:, None] * freqs_j[None, :]
            cos = np.asarray(jnp.cos(ang_j), dtype=np.float32).T
            sin = np.asarray(jnp.sin(ang_j), dtype=np.float32).T
    except Exception:
        pass
    if cos is None:
        j = np.arange(32, dtype=np.float64)
        freqs = (1.0 / (500.0 ** (j / 32.0))).astype(np.float32)
        t = np.arange(T, dtype=np.float32)
        ang = (t[None, :] * freqs[:, None]).astype(np.float32)
        ang64 = ang.astype(np.float64)
        cos = np.cos(ang64).astype(np.float32)
        sin = np.sin(ang64).astype(np.float32)
    c["rope_cos"] = np.tile(cos, (4, 1)).astype(np.float32)
    c["rope_sin"] = np.concatenate([-sin, sin, -sin, sin], axis=0).astype(np.float32)
    R = np.zeros((128, 128), dtype=np.float64)
    for blk in range(2):
        o = blk * 64
        for i in range(32):
            R[o + i, o + 32 + i] = 1.0
            R[o + 32 + i, o + i] = 1.0
    c["rperm"] = R.astype(BF)
    c["ident_bf"] = np.eye(128, dtype=BF)
    c["ident_f32"] = np.eye(128, dtype=np.float32)
    s8q = np.zeros((128, 8), dtype=BF)
    s8k = np.zeros((128, 8), dtype=BF)
    s8q[0:64, 0] = 1
    s8q[64:128, 1] = 1
    s8k[0:64, 4] = 1
    s8k[64:128, 5] = 1
    c["stats8q"] = s8q
    c["stats8k"] = s8k
    c["ones_1x64_f32"] = np.ones((1, 64), dtype=np.float32)
    c["ones_1x64_bf"] = np.ones((1, 64), dtype=BF)
    c["negq_1x128"] = np.full((1, 128), -ATT_SCALE / 2.0, dtype=np.float32)
    c["inv1024_1x128"] = np.full((1, 128), 1.0 / 1024.0, dtype=np.float32)
    c["ones_128x1_f32"] = np.ones((128, 1), dtype=np.float32)
    c["ones_1x128_bf"] = np.ones((1, 128), dtype=BF)
    c["eps_128x1"] = np.full((128, 1), EPS, dtype=np.float32)
    sel = np.zeros((8, 8 * 128), dtype=BF)
    for e in range(8):
        sel[e, e * 128:(e + 1) * 128] = 1
    c["sel8"] = sel
    return c


DT_OF = {np.dtype(np.float32): F32, np.dtype(BF): BF16, np.dtype(np.int32): I32}


def load_consts(P, st, ctx, names):
    nc = P.nc
    arrays = st["carrays"]
    for name in names:
        if name in st["c"]:
            continue
        arr = arrays[name]
        h = nc.inline_tensor(np.ascontiguousarray(arr), "c_" + name)
        buf = P.sbuf(ctx, "cs_" + name, list(arr.shape), DT_OF[arr.dtype])
        P.dma("sync", buf.v, DView(h.ap()))
        st["c"][name] = buf


def unload_consts(st, names):
    for name in names:
        st["c"].pop(name, None)



def load_mat_bf(P, ctx, name, dram_ap, R, C, col0=0, cols=None, n_dma=1):
    cols = C if cols is None else cols
    rch = R // 128
    buf = P.sbuf(ctx, name, [128, rch * cols], BF16)
    src = dram_ap[:, col0:col0 + cols] if (col0 or cols != C) else dram_ap
    src = src.rearrange("(r p) c -> p r c", p=128)
    dst = buf.v.with_ap(buf.h.ap().rearrange("p (r c) -> p r c", c=cols))
    step = max(1, rch // n_dma)
    i = 0
    while i < rch:
        j = min(rch, i + step)
        P.dma("gpsimd",
              View(buf, 0, 128, i * cols, j * cols).with_ap(dst.ap[:, i:j, :]),
              DView(src[:, i:j, :], name=dram_ap.tensor.name))
        i = j
    return buf



def emit_load_x(P, st, x_dram, kind):
    nc = P.nc
    xr, xb = st["xr"], st["xb"]
    if kind == "feat":
        P.dma("sync", xr.v, DView(x_dram))
        for cc in range(CH):
            P.act(xb[:, cc * T:(cc + 1) * T], xr[:, cc * T:(cc + 1) * T], AF.Copy)
        return
    with ExitStack() as ctx:
        xtok = P.sbuf(ctx, "xtok", [128, TT * D], F32)
        P.dma("sync",
              xtok.v.with_ap(xtok.h.ap().rearrange("p (t d) -> p t d", d=D)),
              DView(x_dram.rearrange("(t p) d -> p t d", p=128)))
        ps = [P.psum(ctx, f"pstr{i}", [128, 128]) for i in range(2)]
        idf = st["c"]["ident_f32"]
        for tt in range(TT):
            for cc in range(CH):
                p = ps[(tt * CH + cc) % 2]
                P.mm(p.v, xtok[:, tt * D + cc * 128: tt * D + cc * 128 + 128],
                     idf.v)
                P.copy("vector",
                       xr[:, cc * T + tt * 128: cc * T + tt * 128 + 128], p.v)
                P.act(xb[:, cc * T + tt * 128: cc * T + tt * 128 + 128], p.v,
                      AF.Copy)


def emit_store_x(P, st, out_dram, kind):
    nc = P.nc
    xr = st["xr"]
    if kind == "feat":
        P.dma("sync", DView(out_dram), xr.v)
        return
    with ExitStack() as ctx:
        otok = P.sbuf(ctx, "otok", [128, TT * D], F32)
        ps = [P.psum(ctx, f"psot{i}", [128, 128]) for i in range(2)]
        idf = st["c"]["ident_f32"]
        for tt in range(TT):
            for cc in range(CH):
                p = ps[(tt * CH + cc) % 2]
                P.mm(p.v, xr[:, cc * T + tt * 128: cc * T + tt * 128 + 128],
                     idf.v)
                P.copy("vector",
                       otok[:, tt * D + cc * 128: tt * D + cc * 128 + 128], p.v)
        P.dma("sync", DView(out_dram.rearrange("(t p) d -> p t d", p=128)),
              otok.v)


def emit_postln(P, st):
    nc = P.nc
    c = st["c"]
    xr, xb = st["xr"], st["xb"]
    with ExitStack() as ctx:
        sq = [P.sbuf(ctx, f"ln_sq{i}", [128, TC], F32) for i in range(2)]
        st1 = [P.sbuf(ctx, f"ln_s1{i}", [1, TC], F32) for i in range(2)]
        st2 = [P.sbuf(ctx, f"ln_s2{i}", [1, TC], F32) for i in range(2)]
        bcm = P.sbuf(ctx, "ln_bcm", [128, TC], F32)
        bc2 = P.sbuf(ctx, "ln_bc2", [128, TC], F32)
        tmpm = P.sbuf(ctx, "ln_tmpm", [128, TC], F32)
        var = P.sbuf(ctx, "ln_var", [128, TC], F32)
        lnv = P.sbuf(ctx, "ln_lnv", [128, TC], F32)
        rstd = P.sbuf(ctx, "ln_rstd", [128, TC], F32)
        tmp = [P.sbuf(ctx, f"ln_tmp{i}", [128, TC], F32) for i in range(2)]
        ps1 = [P.psum(ctx, f"ln_ps1{i}", [1, TC]) for i in range(2)]
        ps2 = [P.psum(ctx, f"ln_ps2{i}", [1, TC]) for i in range(2)]
        psb = P.psum(ctx, "ln_psb", [128, TC])
        psb2 = P.psum(ctx, "ln_psb2", [128, TC])
        ones = c["ones_128x1_f32"]

        def stats(tc):
            t0 = tc * TC
            p1, p2 = ps1[tc % 2], ps2[tc % 2]
            for cc in range(CH):
                src = xr[:, cc * T + t0: cc * T + t0 + TC]
                P.act(sq[cc % 2].v, src, AF.Square)
                P.mm(p1.v, ones.v, src, start=(cc == 0), stop=(cc == CH - 1))
                P.mm(p2.v, ones.v, sq[cc % 2].v, start=(cc == 0),
                     stop=(cc == CH - 1))

        def chain(tc):
            t0 = tc * TC
            P.act(st1[tc % 2].v, ps1[tc % 2].v, AF.Copy)
            P.act(st2[tc % 2].v, ps2[tc % 2].v, AF.Copy)
            P.mm(psb.v, c["inv1024_1x128"].v, st1[tc % 2].v)
            P.mm(psb2.v, c["inv1024_1x128"].v, st2[tc % 2].v)
            P.act(bcm.v, psb.v, AF.Copy)
            P.act(bc2.v, psb2.v, AF.Copy)
            P.tt("vector", tmpm.v, bcm.v, bcm.v, OP.mult)
            P.tt("vector", var.v, bc2.v, tmpm.v, OP.subtract)
            P.act(lnv.v, var.v, AF.Ln, bias=c["eps_128x1"].v)
            P.act(rstd.v, lnv.v, AF.Exp, scale=-0.5)
            for cc in range(CH):
                xv = xr[:, cc * T + t0: cc * T + t0 + TC]
                eng = "vector" if cc % 2 == 0 else "gpsimd"
                tb = tmp[cc % 2]
                P.tt(eng, tb.v, xv, bcm.v, OP.subtract)
                P.tt(eng, xv, tb.v, rstd.v, OP.mult)
                P.act(xb[:, cc * T + t0: cc * T + t0 + TC], xv, AF.Copy)

        stats(0)
        for tc in range(1, NTC):
            stats(tc)
            chain(tc - 1)
        chain(NTC - 1)


def emit_attn(P, st, layer, W):
    nc = P.nc
    c = st["c"]
    xr, xb = st["xr"], st["xb"]
    rope = (layer == 0)
    with ExitStack() as ctx:
        if rope:
            load_consts(P, st, ctx, ["rope_cos", "rope_sin"])
        if layer == 0:
            bstage = P.sbuf(ctx, "bstage", [24, 128], F32)
            P.dma("sync", bstage.v,
                  DView(W["b_qkv"].rearrange("(a p) -> a p", p=128)))
            bqT = P.sbuf(ctx, "bqT", [128, 24], F32)
            bq_bf = P.sbuf(ctx, "bq_bf", [128, 24], BF16)
            with ExitStack() as bctx:
                psbq = P.psum(bctx, "psbq", [128, 24])
                P.mm(psbq.v, bstage.v, c["ident_f32"][0:24, 0:24])
                P.act(bqT.v, psbq.v, AF.Copy)
            P.act(bq_bf.v, bqT.v, AF.Copy)
            bo_dram = W["b_out"]
        else:
            bo_dram = W["bo"]
        bostage = P.sbuf(ctx, "bostage", [8, 128], F32)
        P.dma("sync", bostage.v,
              DView(bo_dram.rearrange("(a p) -> a p", p=128)))
        boT = P.sbuf(ctx, "boT", [128, 8], F32)
        with ExitStack() as bctx:
            psbo = P.psum(bctx, "psbo", [128, 8])
            P.mm(psbo.v, bostage.v, c["ident_f32"][0:8, 0:8])
            P.act(boT.v, psbo.v, AF.Copy)

        Oall = P.sbuf(ctx, "Oall", [128, CH * T], BF16)
        biasp = [P.sbuf(ctx, f"biasp{i}", [128, 1], F32) for i in range(2)]

        if layer == 0:
            wsrc = {"q": (W["wqkv"], 0), "k": (W["wqkv"], D),
                    "v": (W["wqkv"], 2 * D)}
        else:
            wsrc = {"q": (W["wq"], 0), "k": (W["wk"], 0), "v": (W["wv"], 0)}

        with ExitStack() as pctx:
            wqb = [P.sbuf(pctx, f"wqb{i}", [128, CH * 128], BF16) for i in range(2)]
            wkb = [P.sbuf(pctx, f"wkb{i}", [128, CH * 128], BF16) for i in range(2)]
            wvb = [P.sbuf(pctx, "wvb0", [128, CH * 128], BF16)] * 2
            QT = [P.sbuf(pctx, f"QT{i}", [128, T], BF16) for i in range(2)]
            KT = [P.sbuf(pctx, f"KT{i}", [128, T], BF16) for i in range(2)]
            VP = [P.sbuf(pctx, f"VP{i}", [128, TT * 130 + 64], BF16) for i in range(2)]
            VT = P.sbuf(pctx, "VT", [128, TC], BF16)
            qtmp = P.sbuf(pctx, "qtmp", [128, TC], BF16)
            rtmp = P.sbuf(pctx, "rtmp", [128, TC], BF16)
            sqb = [P.sbuf(pctx, "attsq0", [128, TC], BF16)] * 2
            smax8 = P.sbuf(pctx, "smax8", [8, 8], F32)
            sm1 = P.sbuf(pctx, "sm1", [8, 1], F32)
            srow8 = P.sbuf(pctx, "srow8", [1, 8], F32)
            mqk = P.sbuf(pctx, "mqk", [1, 2], F32)
            mqk1 = P.sbuf(pctx, "mqk1", [1, 1], F32)
            rec_sb = P.sbuf(pctx, "rec_sb", [128, TC], F32)
            Pa = [P.sbuf(pctx, f"Pa{i}", [128, 2 * TC], BF16) for i in range(3)]
            sra = P.sbuf(pctx, "sra", [1, T], BF16)
            srb = P.sbuf(pctx, "srb", [1, T], BF16)
            for i in range(2):
                P.memset("vector", VP[i].v, 1.0)

            def _wload(key, buf, hp):
                dram, off = wsrc[key]
                base = off + hp * 128
                src = dram[:, base:base + 128].rearrange(
                    "(r p) c -> p r c", p=128)
                dst = buf.v.with_ap(
                    buf.h.ap().rearrange("p (r c) -> p r c", c=128))
                P.dma("gpsimd", dst, DView(src, name=dram.tensor.name))

            def load_slice_qk(par, hp):
                _wload("q", wqb[par], hp)
                _wload("k", wkb[par], hp)

            def load_slice_v(hp):
                _wload("v", wvb[0], hp)

            def gen_phase_a(hp):
                par = hp % 2
                wq, wk, wv = wqb[par], wkb[par], wvb[par]
                qt, kt, vp = QT[par], KT[par], VP[par]
                with ExitStack() as actx:
                    psA = [P.psum(actx, f"psA{i}", [128, TC]) for i in range(2)]
                    for tc in range(NTC):
                        t0 = tc * TC
                        for which in (0, 1, 2):
                            wmat = (wq, wk, wv)[which]
                            pq = psA[which % 2]
                            for cc in range(CH):
                                P.mm(pq.v, wmat[:, cc * 128:(cc + 1) * 128],
                                     xb[:, cc * T + t0: cc * T + t0 + TC],
                                     start=(cc == 0), stop=(cc == CH - 1))
                                yield
                            if which == 2:
                                P.copy("vector", VT.v, pq.v)
                                yield
                                for j in range(TC // 128):
                                    tt = tc * (TC // 128) + j
                                    pv = psA[j % 2]
                                    P.mm(View(pv, 0, 128, 0, 128),
                                         VT[:, j * 128:(j + 1) * 128],
                                         c["ident_bf"].v)
                                    yield
                                    P.copy("vector",
                                           View(vp, 0, 128, tt * 130, tt * 130 + 64),
                                           View(pv, 0, 128, 0, 64))
                                    yield
                                    P.copy("vector",
                                           View(vp, 0, 128, tt * 130 + 65, tt * 130 + 129),
                                           View(pv, 0, 128, 64, 128))
                                    yield
                                continue
                            dstT = qt if which == 0 else kt
                            if layer == 0:
                                bia = bqT[:, (8 * which + hp):(8 * which + hp + 1)]
                            else:
                                bia = None
                            ee = "vector"
                            e2 = "vector"
                            if rope:
                                if bia is not None:
                                    P.ts("vector", qtmp.v, pq.v, bia, OP.add)
                                else:
                                    P.copy("vector", qtmp.v, pq.v)
                                yield
                                P.mm(pq.v, c["rperm"].v, qtmp.v)
                                yield
                                P.tt(ee, dstT[:, t0:t0 + TC], qtmp.v,
                                     c["rope_cos"][:, t0:t0 + TC], OP.mult)
                                yield
                                P.tt("vector", rtmp.v, pq.v,
                                     c["rope_sin"][:, t0:t0 + TC], OP.mult)
                                yield
                                P.tt(ee, dstT[:, t0:t0 + TC],
                                     dstT[:, t0:t0 + TC], rtmp.v, OP.add)
                                yield
                            else:
                                if bia is not None:
                                    P.ts("vector", dstT[:, t0:t0 + TC], pq.v,
                                         bia, OP.add)
                                else:
                                    P.copy("vector", dstT[:, t0:t0 + TC],
                                           pq.v)
                                yield
                            sq = sqb[which]
                            P.tt(e2, sq.v, dstT[:, t0:t0 + TC],
                                 dstT[:, t0:t0 + TC], OP.mult)
                            yield
                            P.mm(View(psA[which % 2], 0, 8, 0, TC),
                                 c["stats8q" if which == 0 else "stats8k"].v,
                                 sq.v)
                            yield
                            P.red("vector",
                                  smax8[:, (4 * which + tc):(4 * which + tc + 1)],
                                  View(psA[which % 2], 0, 8, 0, TC), OP.max)
                            yield
                    P.red("vector", sm1.v, smax8[:, 0:8], OP.max)
                    yield
                    P.mm(View(psA[0], 0, 1, 0, 8), sm1.v, c["ident_f32"][0:8, 0:8])
                    yield
                    P.copy("vector", srow8.v, View(psA[0], 0, 1, 0, 8))
                    yield
                    P.tt("vector", mqk.v, srow8[:, 0:2], srow8[:, 4:6], OP.add)
                    yield
                    P.red("vector", mqk1.v, mqk.v, OP.max)
                    yield
                    P.mm(View(psA[1], 0, 128, 0, 1), c["negq_1x128"].v, mqk1.v)
                    yield
                    P.copy("vector", biasp[par].v, View(psA[1], 0, 128, 0, 1))
                    yield

            def run_core(hp, filler):
                par = hp % 2
                qt, kt, vp = QT[par], KT[par], VP[par]

                def step(n_pe=2, cap=8):
                    if filler is None:
                        return
                    pe0 = P.cnt["Et"]
                    for _ in range(cap):
                        if next(filler, "END") == "END":
                            break
                        if P.cnt["Et"] - pe0 >= n_pe:
                            break
                with ExitStack() as cctx:
                    psOa = P.psum(cctx, "psOa", [128, TC])
                    psOb = P.psum(cctx, "psOb", [128, TC])
                    psS = [P.psum(cctx, f"psS{i}", [128, 2 * TC]) for i in range(2)]
                    for tc in range(NTC):
                        t0 = tc * TC

                        def s_mm(kc):
                            k0 = kc * 128
                            ps = psS[kc % 2]
                            P.mm(View(ps, 0, 128, 0, TC),
                                 View(kt, 0, 64, k0, k0 + 128),
                                 View(qt, 0, 64, t0, t0 + TC))
                            P.mm(View(ps, 0, 128, TC, 2 * TC),
                                 View(kt, 64, 128, k0, k0 + 128),
                                 View(qt, 64, 128, t0, t0 + TC))
                        s_mm(0)
                        P.act(Pa[0].v, psS[0].v, AF.Exp,
                              bias=biasp[par][:, 0:1], scale=ATT_SCALE)
                        s_mm(1)
                        for kc in range(TT):
                            if kc + 1 < TT:
                                P.act(Pa[(kc + 1) % 3].v, psS[(kc + 1) % 2].v,
                                      AF.Exp, bias=biasp[par][:, 0:1],
                                      scale=ATT_SCALE)
                            if kc + 2 < TT:
                                s_mm(kc + 2)
                            step(2)
                            pa = Pa[kc % 3]
                            P.mm(psOa.v,
                                 View(vp, 0, 128, kc * 130, kc * 130 + 128),
                                 View(pa, 0, 128, 0, TC),
                                 start=(kc == 0), stop=(kc == TT - 1))
                            P.mm(psOb.v,
                                 View(vp, 0, 128, kc * 130 + 65, kc * 130 + 193),
                                 View(pa, 0, 128, TC, 2 * TC),
                                 start=(kc == 0), stop=(kc == TT - 1))
                        ob = hp * T + t0
                        P.copy("vector", View(Oall, 0, 64, ob, ob + TC),
                               View(psOa, 0, 64, 0, TC))
                        P.copy("vector", sra[:, t0:t0 + TC],
                               View(psOa, 64, 65, 0, TC))
                        P.copy("vector", View(Oall, 64, 128, ob, ob + TC),
                               View(psOb, 0, 64, 0, TC))
                        P.copy("vector", srb[:, t0:t0 + TC],
                               View(psOb, 64, 65, 0, TC))
                        step(4)
                if filler is not None:
                    for _ in filler:
                        pass
                with ExitStack() as dctx:
                    psBc = P.psum(dctx, "psBc", [128, TC], side="right")
                    for tc in range(NTC):
                        t0 = tc * TC
                        ob = hp * T + t0
                        P.mm(View(psBc, 0, 64, 0, TC),
                             c["ones_1x64_bf"].v, sra[:, t0:t0 + TC])
                        P.mm(View(psBc, 64, 128, 0, TC),
                             c["ones_1x64_bf"].v, srb[:, t0:t0 + TC])
                        P.recip(rec_sb.v, psBc.v)
                        for p0 in (0, 64):
                            P.tt("vector", View(Oall, p0, p0 + 64, ob, ob + TC),
                                 View(Oall, p0, p0 + 64, ob, ob + TC),
                                 View(rec_sb, p0, p0 + 64, 0, TC), OP.mult)

            load_slice_qk(0, 0)
            load_slice_qk(1, 1)
            load_slice_v(0)
            g = gen_phase_a(0)
            for _ in g:
                pass
            for hp in range(8):
                if hp + 2 < 8:
                    load_slice_qk(hp % 2, hp + 2)
                if hp + 1 < 8:
                    load_slice_v(hp + 1)
                filler = gen_phase_a(hp + 1) if hp + 1 < 8 else None
                run_core(hp, filler)

        wout = load_mat_bf(P, ctx, "woutb", W["wout" if layer == 0 else "wo"],
                           D, D, n_dma=2)
        if layer == 0:
            beff = P.sbuf(ctx, "beff", [128, 8], F32)
            with ExitStack() as bctx:
                psbe = P.psum(bctx, "psbe", [128, 8])
                for f in range(8):
                    for cc in range(CH):
                        P.mm(View(psbe, 0, 128, f, f + 1),
                             wout[:, cc * D + f * 128: cc * D + (f + 1) * 128],
                             bq_bf[:, 16 + cc:17 + cc],
                             start=(cc == 0), stop=(cc == CH - 1))
                P.tt("vector", beff.v, View(psbe, 0, 128, 0, 8), boT.v, OP.add)
        else:
            beff = boT
        with ExitStack() as octx:
            psY = [P.psum(octx, f"psY{i}", [128, TC]) for i in range(2)]
            for tc in range(NTC):
                t0 = tc * TC
                for f in range(8):
                    py = psY[f % 2]
                    for cc in range(CH):
                        P.mm(py.v, wout[:, cc * D + f * 128: cc * D + (f + 1) * 128],
                             View(Oall, 0, 128, cc * T + t0, cc * T + t0 + TC),
                             start=(cc == 0), stop=(cc == CH - 1))
                    P.stt("vector", xr[:, f * T + t0: f * T + t0 + TC],
                          py.v, beff[:, f:f + 1],
                          xr[:, f * T + t0: f * T + t0 + TC], OP.add, OP.add)
    if rope:
        unload_consts(st, ["rope_cos", "rope_sin"])


def emit_ffn(P, st, W):
    nc = P.nc
    c = st["c"]
    xr, xb = st["xr"], st["xb"]
    DEQ2 = (1.0 / 127.0) ** 2
    HH = HID // 2
    NHC = HH // 128
    for half in range(2):
        with ExitStack() as ctx:
            wgT = P.sbuf(ctx, "wgT", [128, CH * HH], BF16)
            wuT = P.sbuf(ctx, "wuT", [128, CH * HH], BF16)
            wdT = P.sbuf(ctx, "wdT", [128, NHC * D], BF16)
            hbuf = P.sbuf(ctx, "hbuf", [128, NHC * TC], BF16)
            gs = [P.sbuf(ctx, f"gs{i}", [128, TC], BF16) for i in range(2)]
            for dst, scr in ((wgT, st["scr_wg"]), (wuT, st["scr_wu"])):
                for cc in range(CH):
                    P.dma("sync",
                          dst[:, cc * HH:(cc + 1) * HH],
                          DView(scr.ap()[half * HH:(half + 1) * HH,
                                         cc * 128:(cc + 1) * 128],
                                name=scr.name),
                          transpose=True)
            for hc in range(NHC):
                P.dma("sync",
                      wdT[:, hc * D:(hc + 1) * D],
                      DView(st["scr_wd"].ap()[:, half * HH + hc * 128:
                                              half * HH + (hc + 1) * 128],
                            name=st["scr_wd"].name),
                      transpose=True)
            with ExitStack() as cctx:
                psG = [P.psum(cctx, f"psG{i}", [128, TC]) for i in range(2)]
                psU = [P.psum(cctx, f"psU{i}", [128, TC]) for i in range(2)]
                psY = [P.psum(cctx, f"psYf{i}", [128, TC]) for i in range(2)]
                for tc in range(NTC):
                    t0 = tc * TC
                    for hc in range(NHC):
                        pg, pu = psG[hc % 2], psU[hc % 2]
                        for cc in range(CH):
                            P.mm(pg.v, wgT[:, cc * HH + hc * 128: cc * HH + (hc + 1) * 128],
                                 xb[:, cc * T + t0: cc * T + t0 + TC],
                                 start=(cc == 0), stop=(cc == CH - 1))
                        for cc in range(CH):
                            P.mm(pu.v, wuT[:, cc * HH + hc * 128: cc * HH + (hc + 1) * 128],
                                 xb[:, cc * T + t0: cc * T + t0 + TC],
                                 start=(cc == 0), stop=(cc == CH - 1))
                        g = gs[hc % 2]
                        P.act(g.v, pg.v, AF.Silu, scale=1.0 / 127.0)
                        P.tt("vector", hbuf[:, hc * TC:(hc + 1) * TC],
                             g.v, pu.v, OP.mult)
                    for f in range(CH):
                        py = psY[f % 2]
                        for hc in range(NHC):
                            P.mm(py.v, wdT[:, hc * D + f * 128: hc * D + (f + 1) * 128],
                                 hbuf[:, hc * TC:(hc + 1) * TC],
                                 start=(hc == 0), stop=(hc == NHC - 1))
                        P.stt("vector", xr[:, f * T + t0: f * T + t0 + TC],
                              py.v, DEQ2, xr[:, f * T + t0: f * T + t0 + TC],
                              OP.mult, OP.add)


def emit_moe(P, st, W):
    nc = P.nc
    c = st["c"]
    xr, xb = st["xr"], st["xb"]
    with ExitStack() as ctx:
        gstage = P.sbuf(ctx, "mgstage", [8, D], F32)
        P.dma("sync", gstage.v, DView(W["gate_w"]))
        gwT = P.sbuf(ctx, "gwT", [128, CH * 8], BF16)
        gbstage = P.sbuf(ctx, "gbstage", [1, 8], F32)
        P.dma("sync", gbstage.v, DView(W["gate_b"][None, :], name="gate_b"))
        with ExitStack() as tctx:
            pst = P.psum(tctx, "mpst", [128, 8])
            for cc in range(CH):
                P.mm(pst.v, gstage[0:8, cc * 128:(cc + 1) * 128],
                     c["ident_f32"][0:8, 0:8])
                P.copy("vector", gwT[:, cc * 8:(cc + 1) * 8], pst.v)
            psgb = P.psum(tctx, "psgb", [8, 1])
            P.mm(psgb.v, gbstage.v, c["ident_f32"][0:1, 0:1])
            gbT = P.sbuf(ctx, "gbT", [8, 1], F32)
            P.act(gbT.v, psgb.v, AF.Copy)
        eb = P.sbuf(ctx, "ebb", [8, D], BF16)
        P.dma("gpsimd", eb.v, DView(W["expert_b"]))
        wT = P.sbuf(ctx, "wTmoe", [8, T], BF16)
        with ExitStack() as gctx:
            psL = P.psum(gctx, "psL", [8, TC])
            psG2 = P.psum(gctx, "psG2", [128, 8])
            psW2 = P.psum(gctx, "psW2", [8, 128])
            Lsb = P.sbuf(gctx, "Lsb", [8, TC], F32)
            gt = P.sbuf(gctx, "gt", [128, 8], F32)
            negm = P.sbuf(gctx, "negm", [128, 1], F32)
            ex = P.sbuf(gctx, "ex", [128, 8], F32)
            sume = P.sbuf(gctx, "sume", [128, 1], F32)
            rec = P.sbuf(gctx, "rec", [128, 1], F32)
            gg = P.sbuf(gctx, "gg", [128, 8], F32)
            m1 = P.sbuf(gctx, "m1", [128, 8], F32)
            gm = P.sbuf(gctx, "gm", [128, 8], F32)
            m2 = P.sbuf(gctx, "m2", [128, 1], F32)
            msk = P.sbuf(gctx, "msk", [128, 8], F32)
            wtk = P.sbuf(gctx, "wtk", [128, 8], BF16)
            for tc in range(NTC):
                t0 = tc * TC
                for cc in range(CH):
                    P.mm(psL.v, gwT[:, cc * 8:(cc + 1) * 8],
                         xb[:, cc * T + t0: cc * T + t0 + TC],
                         start=(cc == 0), stop=(cc == CH - 1))
                P.act(Lsb.v, psL.v, AF.Copy, bias=gbT.v)
                for ttl in range(TC // 128):
                    P.mm(psG2.v, Lsb[0:8, ttl * 128:(ttl + 1) * 128],
                         c["ident_f32"][0:8, 0:8])
                    P.copy("vector", gt.v, psG2.v)
                    P.red("vector", negm.v, gt.v, OP.max, negate=True)
                    P.act(ex.v, gt.v, AF.Exp, bias=negm.v, accum=sume.v)
                    P.recip(rec.v, sume.v)
                    P.ts("vector", gg.v, ex.v, rec.v, OP.mult)
                    P.memset("vector", m1.v, -1e30)
                    P.red("vector", m1[:, 0:1], gg.v, OP.max)
                    P.match_replace(gm.v, m1.v, gg.v, -1e30)
                    P.red("vector", m2.v, gm.v, OP.max)
                    P.ts("vector", msk.v, gg.v, m2.v, OP.is_ge)
                    P.tt("vector", wtk.v, gg.v, msk.v, OP.mult)
                    P.mm(psW2.v, wtk.v, c["ident_bf"].v)
                    P.copy("vector",
                           wT[0:8, t0 + ttl * 128: t0 + (ttl + 1) * 128],
                           View(psW2, 0, 8, 0, 128))
        ewT = [P.sbuf(ctx, f"ewT{i}", [128, CH * D], BF16) for i in range(4)]
        bca = P.sbuf(ctx, "bca", [128, 2 * T], BF16)
        xe = P.sbuf(ctx, "xe", [128, 2 * CH * TC], BF16)
        scr_ew = st["scr_ew"]

        def load_pair(pg):
            for i, e in enumerate((2 * pg, 2 * pg + 1)):
                slot = (2 * pg + i) % 4
                for cc in range(CH):
                    P.dma("sync",
                          ewT[slot][:, cc * D:(cc + 1) * D],
                          DView(scr_ew.ap()[e * D:(e + 1) * D,
                                            cc * 128:(cc + 1) * 128],
                                name=scr_ew.name),
                          transpose=True)

        load_pair(0)
        for pg in range(4):
            e0, e1 = 2 * pg, 2 * pg + 1
            if pg < 3:
                load_pair(pg + 1)
            with ExitStack() as bctx:
                psBc = P.psum(bctx, "psBcM", [128, TC])
                for i, e in enumerate((e0, e1)):
                    for tc in range(NTC):
                        t0 = tc * TC
                        P.mm(psBc.v, c["sel8"][0:8, e * 128:(e + 1) * 128],
                             wT[0:8, t0:t0 + TC])
                        P.act(bca[:, i * T + t0: i * T + t0 + TC], psBc.v,
                              AF.Copy)
            with ExitStack() as mctx:
                psM = [P.psum(mctx, f"psM{i}", [128, TC]) for i in range(CH)]
                for tc in range(NTC):
                    t0 = tc * TC
                    for i in range(2):
                        for cc in range(CH):
                            P.tt("vector",
                                 xe[:, (i * CH + cc) * TC:(i * CH + cc + 1) * TC],
                                 xb[:, cc * T + t0: cc * T + t0 + TC],
                                 bca[:, i * T + t0: i * T + t0 + TC], OP.mult)
                        ew = ewT[(2 * pg + i) % 4]
                        for f in range(CH):
                            for cc in range(CH):
                                P.mm(psM[f].v,
                                     ew[:, cc * D + f * 128: cc * D + (f + 1) * 128],
                                     xe[:, (i * CH + cc) * TC:(i * CH + cc + 1) * TC],
                                     start=(i == 0 and cc == 0),
                                     stop=(pg < 3 and i == 1 and cc == CH - 1))
                    for f in range(CH):
                        if pg == 3:
                            P.mm(psM[f].v, eb[0:8, f * 128:(f + 1) * 128],
                                 wT[0:8, t0:t0 + TC], start=False, stop=True)
                        P.tt("vector", xr[:, f * T + t0: f * T + t0 + TC],
                             psM[f].v, xr[:, f * T + t0: f * T + t0 + TC],
                             OP.add)



STAGE_CONSTS = {
    "l0_attn": ["ident_f32", "ident_bf", "rperm",
                "stats8q", "stats8k", "ones_1x64_f32", "negq_1x128", "ones_1x64_bf"],
    "l0_ffn": ["ident_bf", "inv1024_1x128", "ones_128x1_f32"],
    "l1_attn": ["ident_f32", "ident_bf", "stats8q", "stats8k", "ones_1x64_bf",
                "ones_1x64_f32", "negq_1x128"],
    "l1_moe": ["ident_f32", "ident_bf", "ones_1x128_bf", "sel8"],
    "ln": ["inv1024_1x128", "ones_128x1_f32", "eps_128x1"],
    "io": ["ident_f32"],
}


def build_program(stages, in_kind, out_kind):
    nc = bass.Bass("TRN2", target_bir_lowering=False)
    need = set(STAGE_CONSTS["io"]) | set(STAGE_CONSTS["ln"])
    for s in stages:
        need |= set(STAGE_CONSTS[s])
    ins = {}
    if in_kind == "tok":
        x_in = nc.dram_tensor("x_in", [T, D], F32, kind="ExternalInput").ap()
        ins["x_in"] = "x_tok"
    else:
        x_in = nc.dram_tensor("h_in", [128, CH * T], F32,
                              kind="ExternalInput").ap()
        ins["h_in"] = "handoff"
    W = {}
    def din(name, shape, dt=F32):
        W[name] = nc.dram_tensor(name, shape, dt, kind="ExternalInput").ap()
        ins[name] = "w"
    if "l0_attn" in stages:
        din("wqkv", [D, 3 * D])
        din("b_qkv", [3 * D])
        din("wout", [D, D])
        din("b_out", [D])
    if "l0_ffn" in stages:
        din("w_gate", [HID, D], I32)
        din("w_up", [HID, D], I32)
        din("w_down", [D, HID], I32)
    if "l1_attn" in stages:
        din("wq", [D, D])
        din("wk", [D, D])
        din("wv", [D, D])
        din("wo", [D, D])
        din("bo", [D])
    if "l1_moe" in stages:
        din("gate_w", [NE, D])
        din("gate_b", [NE])
        din("expert_w", [NE, D, D])
        din("expert_b", [NE, D])
    if out_kind == "tok":
        x_out = nc.dram_tensor("x_out", [T, D], F32, kind="ExternalOutput").ap()
    else:
        x_out = nc.dram_tensor("h_out", [128, CH * T], F32,
                               kind="ExternalOutput").ap()

    with ExitStack() as ctx:
        P = Prog(nc, ctx)
        st = {"carrays": make_const_arrays(), "c": {}}
        st["xr"] = P.sbuf(ctx, "xr", [128, CH * T], F32)
        st["xb"] = P.sbuf(ctx, "xb", [128, CH * T], BF16)
        if "l0_ffn" in stages:
            st["scr_wg"] = nc.dram_tensor("scr_wg", [HID, D], BF16, kind="Internal")
            st["scr_wu"] = nc.dram_tensor("scr_wu", [HID, D], BF16, kind="Internal")
            st["scr_wd"] = nc.dram_tensor("scr_wd", [D, HID], BF16, kind="Internal")
            for scr, dram in ((st["scr_wg"], W["w_gate"]),
                              (st["scr_wu"], W["w_up"]),
                              (st["scr_wd"], W["w_down"])):
                P.dma("gpsimd", DView(scr.ap(), name=scr.name), DView(dram))
        if "l1_moe" in stages:
            st["scr_ew"] = nc.dram_tensor("scr_ew", [NE * D, D], BF16,
                                          kind="Internal")
            P.dma("gpsimd",
                  DView(st["scr_ew"].ap(), name=st["scr_ew"].name),
                  DView(W["expert_w"].rearrange("e o h -> (e o) h"),
                        name=W["expert_w"].tensor.name))
        load_consts(P, st, ctx, sorted(need))
        emit_load_x(P, st, x_in, in_kind)
        for s in stages:
            if s == "l0_attn":
                emit_attn(P, st, 0, W)
            elif s == "l0_ffn":
                emit_ffn(P, st, W)
            elif s == "l1_attn":
                emit_attn(P, st, 1, W)
            elif s == "l1_moe":
                emit_moe(P, st, W)
            emit_postln(P, st)
        emit_store_x(P, st, x_out, out_kind)
        P.barrier_all()
        P.realize()
    return nc, ins



STAGE_GROUPS = [["l0_attn", "l0_ffn", "l1_attn", "l1_moe"]]

W_INPUT_MAP = {
    "wqkv": "L0_attn_w_qkv", "b_qkv": "L0_attn_b_qkv",
    "wout": "L0_attn_w_out", "b_out": "L0_attn_b_out",
    "w_gate": "L0_ffn_w_gate", "w_up": "L0_ffn_w_up", "w_down": "L0_ffn_w_down",
    "wq": "L1_attn_Wq", "wk": "L1_attn_Wk", "wv": "L1_attn_Wv",
    "wo": "L1_attn_Wo", "bo": "L1_attn_bo",
    "gate_w": "L1_ffn_gate_w", "gate_b": "L1_ffn_gate_b",
    "expert_w": "L1_ffn_expert_w", "expert_b": "L1_ffn_expert_b",
}


def kernel(**inputs):
    x = np.ascontiguousarray(inputs["L0_attn_x"], dtype=np.float32)
    cores = list(range(8))
    stage_groups = STAGE_GROUPS
    cur = [x[b] for b in range(B)]
    n_groups = len(stage_groups)
    for gi, stages in enumerate(stage_groups):
        in_kind = "tok" if gi == 0 else "feat"
        out_kind = "tok" if gi == n_groups - 1 else "feat"
        nc, ins = build_program(stages, in_kind, out_kind)
        in_maps = []
        for b in range(B):
            m = {}
            for name, kind in ins.items():
                if kind == "x_tok" or kind == "handoff":
                    m[name] = np.ascontiguousarray(cur[b])
                else:
                    m[name] = np.ascontiguousarray(
                        inputs[W_INPUT_MAP[name]])
            in_maps.append(m)
        res = run_bass_kernel_spmd(nc, in_maps, core_ids=cores)
        okey = "x_out" if out_kind == "tok" else "h_out"
        cur = [res.results[b][okey] for b in range(B)]
    out = np.stack(cur, axis=0).astype(np.float32)
    return out
